# version 02
# baseline (speedup 1.0000x reference)
import contextlib
import math
import numpy as np
import ml_dtypes

import concourse.bass as bass
import concourse.mybir as mybir
from concourse.bass_utils import run_bass_kernel_spmd

F32 = mybir.dt.float32
BF16 = mybir.dt.bfloat16
AF = mybir.ActivationFunctionType
ALU = mybir.AluOpType
AX = mybir.AxisListType
NBF = ml_dtypes.bfloat16

S = 2048
D = 1024
NT = S // 128
NC = D // 128

ENGINES = ("tensor", "vector", "scalar", "gpsimd", "sync")


class Prog:
    ESEM = {"tensor": "T", "vector": "V", "scalar": "A", "gpsimd": "G", "sync": "S"}

    def __init__(self, nc):
        self.nc = nc
        self.ops = {e: [] for e in ENGINES}
        self.sem_names = []
        self.count = {}
        self.handles = {}
        self.clock = {e: {} for e in ENGINES}
        self.tokmeta = {}
        self.dma_sems = set()
        for e in ENGINES:
            self.sem(self.ESEM[e])

    def sem(self, name):
        if name not in self.count:
            self.sem_names.append(name)
            self.count[name] = 0
        return name

    def emit(self, eng, fn, wait=(), dma=False, dsem=None):
        own = self.ESEM[eng]
        cl = self.clock[eng]
        wd = {}
        for t in wait:
            if t is None:
                continue
            s, c = t
            if c <= 0 or cl.get(s, 0) >= c:
                continue
            wd[s] = max(wd.get(s, 0), c)
        if eng in ("vector", "scalar", "gpsimd") and                 self.count[own] > cl.get(own, 0):
            wd[own] = self.count[own]
        for s, c in wd.items():
            meta = self.tokmeta.get((s, c))
            if meta:
                for ms, mc in meta.items():
                    if cl.get(ms, 0) < mc:
                        cl[ms] = mc
            if cl.get(s, 0) < c:
                cl[s] = c
        wait_t = tuple(sorted(wd.items()))
        d = 16 if dma else 1
        tsem = own
        if dma:
            tsem = dsem or (own + "d")
            self.sem(tsem)
            self.dma_sems.add(tsem)
        self.count[tsem] += d
        tok = (tsem, self.count[tsem])
        self.tokmeta[tok] = dict(cl)
        self.tokmeta[tok][tsem] = self.count[tsem]

        def thunk(engobj, _fn=fn, _wait=wait_t, _ts=tsem, _d=d):
            for s, c in _wait[:-1]:
                engobj.wait_ge(self.handles[s], c)
            inst = _fn(engobj)
            assert inst is not None, "emit fn returned no instruction"
            if _wait:
                s, c = _wait[-1]
                inst._wait_ge(self.handles[s], c)
            inst.then_inc(self.handles[_ts], _d)
        self.ops[eng].append(thunk)
        return tok

    def now(self, eng):
        return (self.ESEM[eng], self.count[self.ESEM[eng]])

    def build(self):
        nc = self.nc
        stack = contextlib.ExitStack()
        for name in self.sem_names:
            self.handles[name] = stack.enter_context(nc.semaphore(name))
        block = stack.enter_context(nc.Block())
        for e in ENGINES:
            if not self.ops[e]:
                continue

            def make_body(ths):
                def body(engobj):
                    for t in ths:
                        t(engobj)
                return body
            getattr(block, e)(make_body(self.ops[e]))
        stack.close()


class Stage:

    def __init__(self, name):
        self.nc = bass.Bass("TRN2", target_bir_lowering=False, debug=False)
        self.p = Prog(self.nc)
        self.stack = contextlib.ExitStack()
        self.in_names = []

    def inp(self, name, shape, dt=F32):
        self.in_names.append(name)
        return self.nc.dram_tensor(name, list(shape), dt, kind="ExternalInput").ap()

    def outp(self, name, shape, dt=F32):
        return self.nc.dram_tensor(name, list(shape), dt, kind="ExternalOutput").ap()

    def sb(self, name, shape, dt):
        return self.stack.enter_context(self.nc.sbuf_tensor(name, shape, dt))

    def ps(self, name, shape, dt=F32):
        return self.stack.enter_context(self.nc.psum_tensor(name, shape, dt))

    def cvec(self, value):
        if not hasattr(self, "_cvec"):
            self._cvec = {}
        if value not in self._cvec:
            t = self.sb(f"cv{len(self._cvec)}", [128, 1], F32)
            tok = self.p.emit("vector", lambda e, t=t, v=value: e.memset(t[:, :], v))
            self._cvec[value] = (t, tok)
        return self._cvec[value]

    def kc(self):
        self._kcn = getattr(self, "_kcn", 0) + 1
        return f"kc{self._kcn}"

    def drain(self):
        p = self.p
        waits = [(s, p.count[s]) for s in sorted(p.dma_sems)]
        p.emit("sync", lambda e: e.nop(), wait=waits)

    def done(self):
        self.p.build()
        self.stack.close()
        return self



def dma_img_in(st, sbuf, ap, R, C, wait=(), dsem="xi"):
    tok = None
    for c in range(R // 128):
        tok = st.p.emit("sync", lambda e, c=c: e.dma_start(
            out=sbuf[:, c * C:(c + 1) * C], in_=ap[c * 128:(c + 1) * 128, :]),
            wait=wait, dma=True, dsem=dsem)
    return tok


def dma_img_out(st, ap, sbuf, R, C, wait=(), dsem="xo"):
    tok = None
    for c in range(R // 128):
        tok = st.p.emit("sync", lambda e, c=c: e.dma_start(
            out=ap[c * 128:(c + 1) * 128, :], in_=sbuf[:, c * C:(c + 1) * C]),
            wait=wait, dma=True, dsem=dsem)
    return tok


class WStream:

    def __init__(self, st, name, unit_cols, units, nbuf=2, nf32=1, cast=True):
        self.st, self.units, self.nbuf, self.cast = st, units, nbuf, cast
        self.nf32 = nf32
        self.name = name
        dt_out = BF16 if cast else F32
        self.bf = [st.sb(f"{name}_b{j}", [128, unit_cols], dt_out)
                   for j in range(nbuf)]
        self.f32 = [st.sb(f"{name}_f{j}", [128, unit_cols], F32)
                    for j in range(nf32)] if cast else self.bf
        self.issued = 0
        self.rel = {}
        self.ready_tok = {}
        self.cast_tok = {}
        for _ in range(min(nbuf, len(units))):
            self._issue()

    def _issue(self):
        u, p = self.issued, self.st.p
        w = []
        if u >= self.nbuf:
            w.append(self.rel[u - self.nbuf])
        if self.cast:
            wld = [self.cast_tok.get(u - self.nf32)]
            ldt = p.emit("sync", lambda e, u=u: e.dma_start(
                out=self.f32[u % self.nf32][:, :], in_=self.units[u]),
                wait=wld, dma=True, dsem=self.name + str(u % self.nf32))
            ct = p.emit("gpsimd", lambda e, u=u: e.tensor_copy(
                out=self.bf[u % self.nbuf][:, :], in_=self.f32[u % self.nf32][:, :]),
                wait=[ldt] + w)
            self.cast_tok[u] = ct
            self.ready_tok[u] = ct
        else:
            ldt = p.emit("sync", lambda e, u=u: e.dma_start(
                out=self.bf[u % self.nbuf][:, :], in_=self.units[u]),
                wait=w, dma=True, dsem=self.name + str(u % self.nbuf))
            self.ready_tok[u] = ldt
        self.issued += 1

    def ready(self, u):
        return self.ready_tok[u]

    def buf(self, u):
        return self.bf[u % self.nbuf]

    def release(self, u, tok):
        self.rel[u] = tok
        if self.issued < len(self.units):
            self._issue()


class MMQ:

    def __init__(self, st, banks):
        self.st, self.banks = st, banks
        self.free_tok = [None] * len(banks)
        self.g = 0

    def group(self, pairs, evac, first_wait=()):
        p = self.st.p
        nb = len(self.banks)
        b = self.g % nb
        bank = self.banks[b]
        n = len(pairs)
        mm_tok = None
        for k, (lh, rh, osl) in enumerate(pairs):
            w = []
            if k == 0:
                w = [self.free_tok[b]] + list(first_wait)
            out_ap = bank[:, :] if osl is None else osl(bank)
            mm_tok = p.emit("tensor", lambda e, lh=lh, rh=rh, o=out_ap, k=k, n=n:
                            e.matmul(o, lh, rh, start=(k == 0), stop=(k == n - 1)),
                            wait=w)
        tok = mm_tok
        for eng, fn, xw in evac:
            tok = p.emit(eng, lambda e, fn=fn, bk=bank: fn(e, bk),
                         wait=[mm_tok, tok] + list(xw))
        self.free_tok[b] = tok
        self.g += 1
        return tok



def pack_w(W, gc):
    K, N = W.shape
    kc = K // 128
    assert K % 128 == 0 and N % gc == 0, (K, N, gc)
    return np.ascontiguousarray(
        W.reshape(kc, 128, N // gc, gc).transpose(2, 1, 0, 3)
         .reshape(N // gc, 128, kc * gc).astype(np.float32))


def feat_img_np(A):
    R, C = A.shape
    return np.ascontiguousarray(A.reshape(R // 128, 128, C).transpose(1, 0, 2)
                                .reshape(128, (R // 128) * C))


def make_consts():
    c = {}
    ii = np.arange(128)
    c["mask_low"] = np.where(ii[:, None] >= ii[None, :], 0.0, -1e30).astype(np.float32)
    c["mask_upp"] = np.where(ii[:, None] <= ii[None, :], 0.0, -1e30).astype(np.float32)
    c["id_bf"] = np.eye(128, dtype=np.float32).astype(NBF)
    c["id_f32"] = np.eye(128, dtype=np.float32)
    c["ones_col"] = np.ones((128, 1), np.float32)
    c["ones_row"] = np.ones((1, 128), np.float32)
    c["ones8_row"] = np.full((1, 128), 0.125, np.float32)
    i32 = np.arange(32, dtype=np.float32)
    inv_freq = (10000.0 ** (-2.0 * i32 / 64.0))
    pos = np.arange(S, dtype=np.float32)
    ang = pos[:, None] * inv_freq[None, :]
    cos, sin = np.cos(ang), np.sin(ang)
    img = lambda t: feat_img_np(np.ascontiguousarray(t.astype(np.float32)))
    c["r0_cos_q"] = img(cos * 0.125)
    c["r0_sin_q"] = img(sin * 0.125)
    c["r0_cos_k"] = img(cos)
    c["r0_sin_k"] = img(sin)
    half = np.arange(0, 64, 2, dtype=np.float32)
    invf = 1.0 / 10000.0 ** (half / 64.0)
    t = np.arange(S, dtype=np.float32)
    freqs = np.concatenate([t[:, None] * invf[None, :]] * 2, -1)
    base = (half + 0.4 * 64) / (1.4 * 64)
    power = (t - S // 2) / 128.0
    scale = np.concatenate([base[None, :] ** power[:, None]] * 2, -1)
    cf, sf = np.cos(freqs), np.sin(freqs)
    c["r1_cs_q"] = img(cf * scale * 8.0)
    c["r1_sn_q"] = img(sf * scale * 8.0)
    c["r1_cs_k"] = img(cf / scale)
    c["r1_sn_k"] = img(sf / scale)
    return c


def pack_weights(inp):
    w = {}
    for nm in ("W_q", "W_k", "W_v"):
        Wt = np.ascontiguousarray(np.asarray(inp[f"L0_attn_{nm}"]).T)
        w[f"l0_{nm}"] = pack_w(Wt, 512)
    w["l0_W_o"] = pack_w(np.ascontiguousarray(np.asarray(inp["L0_attn_W_o"]).T), 128)
    rg = np.concatenate([np.asarray(inp["L0_ffn_w_router"]),
                         np.asarray(inp["L0_ffn_w_gating"])], axis=1)
    w["l0_rg"] = pack_w(rg, 16)[0]
    wg = np.asarray(inp["L0_ffn_w_gate"])
    wu = np.asarray(inp["L0_ffn_w_up"])
    wd = np.asarray(inp["L0_ffn_w_down"])
    for e in range(8):
        g = pack_w(wg[e], 128)
        u = pack_w(wu[e], 128)
        w[f"moe_gu_{e}"] = np.concatenate([g, u], axis=2)
        w[f"moe_d_{e}"] = pack_w(wd[e], 128).transpose(0, 1, 2)
    for nm in ("Wq", "Wk", "Wv"):
        w[f"l1_{nm}"] = pack_w(np.asarray(inp[f"L1_attn_{nm}"]), 512)
    w["l1_Wo"] = pack_w(np.asarray(inp["L1_attn_Wo"]), 128)
    for nm in ("bq", "bk", "bv"):
        w[f"l1_{nm}"] = np.asarray(inp[f"L1_attn_{nm}"]).reshape(1, 1024)
    w["l1_bo"] = np.ascontiguousarray(
        np.asarray(inp["L1_attn_bo"]).reshape(8, 128).T)
    Wqkv = np.asarray(inp["L1_ffn_Wqkv"])
    Wo = np.asarray(inp["L1_ffn_Wo"])
    W1 = np.asarray(inp["L1_ffn_W1"])
    W2 = np.asarray(inp["L1_ffn_W2"])
    qs = np.asarray(inp["L1_ffn_q_scale"])
    ks = np.asarray(inp["L1_ffn_k_scale"])
    lng = np.asarray(inp["L1_ffn_ln_g"])
    lnb = np.asarray(inp["L1_ffn_ln_b"])
    b1 = np.asarray(inp["L1_ffn_b1"])
    b2 = np.asarray(inp["L1_ffn_b2"])
    FI = 2730
    FP = 2816
    for i in range(6):
        w[f"in_qkv_{i}"] = pack_w(Wqkv[i], 512)
        w[f"in_wo_{i}"] = pack_w(Wo[i], 128)
        W1p = np.zeros((1024, 2 * FP), np.float32)
        W1p[:, :FI] = W1[i][:, :FI]
        W1p[:, FP:FP + FI] = W1[i][:, FI:]
        wa = pack_w(W1p[:, :FP], 128)
        wgt = pack_w(W1p[:, FP:], 128)
        w[f"in_w1_{i}"] = np.concatenate([wa, wgt], axis=2)
        W2p = np.zeros((FP, 1024), np.float32)
        W2p[:FI] = W2[i]
        w[f"in_w2_{i}"] = pack_w(W2p, 128)
        w[f"in_qs_{i}"] = qs[i].reshape(1, 64).astype(np.float32)
        w[f"in_qsr_{i}"] = np.roll(qs[i], -32).reshape(1, 64).astype(np.float32)
        w[f"in_ks_{i}"] = ks[i].reshape(1, 64).astype(np.float32)
        w[f"in_ksr_{i}"] = np.roll(ks[i], -32).reshape(1, 64).astype(np.float32)
        w[f"in_gb_{i}"] = np.stack([lng[i], lnb[i]]).astype(np.float32)
        w[f"in_gpp_{i}"] = np.ascontiguousarray(lng[i].reshape(8, 128).T)
        w[f"in_bpp_{i}"] = np.ascontiguousarray(lnb[i].reshape(8, 128).T)
        b1p = np.zeros((2, FP), np.float32)
        b1p[0, :FI] = b1[i][:FI]
        b1p[1, :FI] = b1[i][FI:]
        w[f"in_b1a_{i}"] = np.ascontiguousarray(b1p[0].reshape(22, 128).T)
        w[f"in_b1g_{i}"] = np.ascontiguousarray(b1p[1].reshape(22, 128).T)
        w[f"in_b2_{i}"] = np.ascontiguousarray(b2[i].reshape(8, 128).T)
    return w

_STAGES = {}


def stage(name):
    def deco(fn):
        def get():
            if name not in _STAGES:
                _STAGES[name] = fn()
            return _STAGES[name]
        get.__name__ = name
        _STAGES["get_" + name] = get
        return get
    return deco


@stage("ln")
def build_ln():
    st = Stage("ln")
    p = st.p
    apx = st.inp("x", (S, D))
    apid = st.inp("id_bf", (128, 128), BF16)
    apoT = st.outp("xnT_bf", (D, S), BF16).rearrange("(c p) t -> p c t", p=128)
    apoF = st.outp("xnT_f32", (D, S), F32).rearrange("(c p) t -> p c t", p=128)
    sid = st.sb("sid", [128, 128], BF16)
    sx = [st.sb(f"sx{j}", [128, D], F32) for j in range(2)]
    scrq = [st.sb(f"scrq{j}", [128, D], F32) for j in range(2)]
    sxn = [st.sb(f"sxn{j}", [128, D], BF16) for j in range(2)]
    stgb = [st.sb(f"stgb{j}", [128, 512], BF16) for j in range(2)]
    stgf = [st.sb(f"stgf{j}", [128, 512], F32) for j in range(2)]
    pst = [st.ps(f"pst{j}", [128, 512], BF16) for j in range(2)]
    V = lambda nm: st.sb(nm, [128, NT], F32)
    ssq, ssum, smu, smu2, svar, sstd, sr, sbb = [V(n) for n in
        ("ssq", "ssum", "smu", "smu2", "svar", "sstd", "sr", "sbb")]
    idt = p.emit("sync", lambda e: e.dma_start(out=sid[:, :], in_=apid[:, :]),
                 dma=True, dsem=st.kc())

    xn_done = {}
    tp_done = {}
    stg_free = {}
    ps_free = {}
    gidx = 0
    for i in range(NT):
        j = i % 2
        ldt = p.emit("sync", lambda e, i=i, j=j: e.dma_start(
            out=sx[j][:, :], in_=apx[i * 128:(i + 1) * 128, :]),
            wait=[xn_done.get(i - 2)], dma=True, dsem=f"sx{j}")
        ci = lambda t, i=i: t[:, i:i + 1]
        t_sq = p.emit("scalar", lambda e, i=i, j=j: e.activation(
            out=scrq[j][:, :], in_=sx[j][:, :], func=AF.Square,
            accum_out=ssq[:, i:i + 1]), wait=[ldt])
        t_sum = p.emit("vector", lambda e, i=i, j=j: e.reduce_sum(
            out=ssum[:, i:i + 1], in_=sx[j][:, :], axis=AX.X), wait=[ldt])
        t_mu = p.emit("scalar", lambda e, i=i: e.mul(
            out=smu[:, i:i + 1], in_=ssum[:, i:i + 1], mul=1.0 / D), wait=[t_sum])
        t_mu2 = p.emit("scalar", lambda e, i=i: e.activation(
            out=smu2[:, i:i + 1], in_=smu[:, i:i + 1], func=AF.Square), wait=[])
        t_var = p.emit("vector", lambda e, i=i: e.scalar_tensor_tensor(
            out=svar[:, i:i + 1], in0=ssq[:, i:i + 1], scalar=1.0 / D,
            in1=smu2[:, i:i + 1], op0=ALU.mult, op1=ALU.subtract),
            wait=[t_sq, t_mu2])
        eps_ap, eps_tok = st.cvec(1e-5)
        t_std = p.emit("scalar", lambda e, i=i, b=eps_ap: e.activation(
            out=sstd[:, i:i + 1], in_=svar[:, i:i + 1], func=AF.Sqrt, bias=b[:, :]),
            wait=[t_var, eps_tok])
        t_r = p.emit("vector", lambda e, i=i: e.reciprocal(
            out=sr[:, i:i + 1], in_=sstd[:, i:i + 1]), wait=[t_std])
        t_b = p.emit("vector", lambda e, i=i: e.scalar_tensor_tensor(
            out=sbb[:, i:i + 1], in0=smu[:, i:i + 1], scalar=-1.0,
            in1=sr[:, i:i + 1], op0=ALU.mult, op1=ALU.mult), wait=[t_mu])
        t_xn = p.emit("scalar", lambda e, i=i, j=j: e.activation(
            out=sxn[j][:, :], in_=sx[j][:, :], func=AF.Identity,
            scale=sr[:, i:i + 1], bias=sbb[:, i:i + 1]),
            wait=[t_b, tp_done.get(i - 2)])
        xn_done[i] = t_xn
        for g in range(2):
            tp = None
            for k in range(4):
                c = 4 * g + k
                tp = p.emit("tensor", lambda e, j=j, c=c, k=k, g=gidx % 2: e.transpose(
                    pst[g][:, k * 128:(k + 1) * 128],
                    sxn[j][:, c * 128:(c + 1) * 128], sid[:, :]),
                    wait=([t_xn, idt, ps_free.get(gidx - 2)] if k == 0 else []))
            t_ev = p.emit("vector", lambda e, g=gidx % 2: e.tensor_copy(
                out=stgb[g][:, :], in_=pst[g][:, :]),
                wait=[tp, stg_free.get(gidx - 2)])
            ps_free[gidx] = t_ev
            t_wd = p.emit("scalar", lambda e, g=gidx % 2: e.copy(
                out=stgf[g][:, :], in_=stgb[g][:, :]), wait=[t_ev])
            d1 = p.emit("sync", lambda e, i=i, g=g, gg=gidx % 2: e.dma_start(
                out=apoT[:, 4 * g:4 * g + 4, i * 128:(i + 1) * 128],
                in_=stgb[gg][:, :].rearrange("p (c t) -> p c t", t=128)),
                wait=[t_ev], dma=True, dsem=f"so{gidx % 2}")
            d2 = p.emit("sync", lambda e, i=i, g=g, gg=gidx % 2: e.dma_start(
                out=apoF[:, 4 * g:4 * g + 4, i * 128:(i + 1) * 128],
                in_=stgf[gg][:, :].rearrange("p (c t) -> p c t", t=128)),
                wait=[t_wd, d1], dma=True, dsem=f"so{gidx % 2}")
            stg_free[gidx] = d2
            tp_done[i] = tp
            gidx += 1
    st.drain()
    return st.done()


@stage("router")
def build_router():
    st = Stage("router")
    p = st.p
    apxT = st.inp("xnT_bf", (D, S), BF16)
    aprg = st.inp("rg", (128, 128))
    apid = st.inp("id_f32", (128, 128))
    apcw = st.outp("cwT", (8, S))
    ximg = st.sb("ximg", [128, NC * S], BF16)
    srgf = st.sb("srgf", [128, 128], F32)
    srgb = st.sb("srgb", [128, 128], BF16)
    sidf = st.sb("sidf", [128, 128], F32)
    slog = st.sb("slog", [16, S], F32)
    stok = [st.sb(f"stok{j}", [128, 16], F32) for j in range(2)]
    sprod = [st.sb(f"sprod{j}", [128, 8], F32) for j in range(2)]
    svs0 = st.sb("svs", [128, 16], F32)
    scw = [st.sb(f"scw{j}", [128, 8], F32) for j in range(2)]
    scT = [st.sb(f"scT{j}", [8, 128], F32) for j in range(2)]
    psL = [st.ps(f"psL{j}", [16, 512]) for j in range(2)]
    psT = [st.ps(f"psT{j}", [128, 16]) for j in range(2)]
    psB = [st.ps(f"psB{j}", [8, 128]) for j in range(2)]

    xt = dma_img_in(st, ximg, apxT, D, S)
    t1 = p.emit("sync", lambda e: e.dma_start(out=srgf[:, :], in_=aprg[:, :]),
                dma=True, dsem=st.kc())
    t2 = p.emit("sync", lambda e: e.dma_start(out=sidf[:, :], in_=apid[:, :]),
                dma=True, dsem=st.kc())
    tc = p.emit("gpsimd", lambda e: e.tensor_copy(out=srgb[:, :], in_=srgf[:, :]),
                wait=[t1])
    mq = MMQ(st, psL)
    ev_toks = []
    for t in range(4):
        pairs = [(srgb[:, k * 16:(k + 1) * 16],
                  ximg[:, k * S + t * 512:k * S + (t + 1) * 512], None)
                 for k in range(NC)]
        tok = mq.group(pairs, [("vector",
                                lambda e, b, t=t: e.tensor_copy(
                                    out=slog[:, t * 512:(t + 1) * 512], in_=b[:, :]),
                                [])],
                       first_wait=[tc, xt])
        ev_toks.append(tok)

    tev_done, e2_done, te2_done, tb_done, cw_dma = {}, {}, {}, {}, {}
    for i in range(NT):
        j = i % 2
        tt = p.emit("tensor", lambda e, i=i, j=j: e.transpose(
            psT[j][:, :], slog[:, i * 128:(i + 1) * 128], sidf[0:16, 0:16]),
            wait=[ev_toks[i // 4], t2, tev_done.get(i - 2)])
        tev = p.emit("vector", lambda e, j=j: e.tensor_copy(
            out=stok[j][:, :], in_=psT[j][:, :]), wait=[tt, e2_done.get(i - 2)])
        tev_done[i] = tev
        sv = lambda k, j=j: svs0[:, 8 * j + k:8 * j + k + 1]
        m1 = p.emit("vector", lambda e, sv=sv, j=j: e.reduce_max(
            out=sv(0), in_=stok[j][:, 0:8], axis=AX.X, negate=True), wait=[tev])
        e1 = p.emit("scalar", lambda e, sv=sv, j=j: e.activation(
            out=stok[j][:, 0:8], in_=stok[j][:, 0:8], func=AF.Exp,
            bias=sv(0), accum_out=sv(2)), wait=[m1])
        m2 = p.emit("vector", lambda e, sv=sv, j=j: e.reduce_max(
            out=sv(1), in_=stok[j][:, 8:16], axis=AX.X, negate=True), wait=[e1])
        m2s = p.emit("vector", lambda e, sv=sv: e.tensor_scalar(
            out=sv(1), in0=sv(1), scalar1=10.0, scalar2=None,
            op0=ALU.mult), wait=[])
        e2 = p.emit("scalar", lambda e, sv=sv, j=j: e.activation(
            out=stok[j][:, 8:16], in_=stok[j][:, 8:16], func=AF.Exp, scale=10.0,
            bias=sv(1), accum_out=sv(3)), wait=[m2s])
        e2_done[i] = e2
        pr = p.emit("vector", lambda e, j=j: e.tensor_tensor(
            out=sprod[j][:, :], in0=stok[j][:, 0:8], in1=stok[j][:, 8:16],
            op=ALU.mult), wait=[e2])
        ts = p.emit("vector", lambda e, sv=sv, j=j: e.reduce_sum(
            out=sv(4), in_=sprod[j][:, :], axis=AX.X), wait=[])
        s12 = p.emit("vector", lambda e, sv=sv: e.scalar_tensor_tensor(
            out=sv(5), in0=sv(2), scalar=1e-9, in1=sv(3),
            op0=ALU.mult, op1=ALU.mult), wait=[e2])
        dn = p.emit("vector", lambda e, sv=sv: e.tensor_add(
            out=sv(6), in0=sv(4), in1=sv(5)), wait=[])
        rd = p.emit("vector", lambda e, sv=sv: e.reciprocal(
            out=sv(7), in_=sv(6)), wait=[])
        cw = p.emit("vector", lambda e, sv=sv, j=j: e.tensor_scalar(
            out=scw[j][:, :], in0=sprod[j][:, :], scalar1=sv(7), scalar2=None,
            op0=ALU.mult), wait=[tb_done.get(i - 2)])
        tb = p.emit("tensor", lambda e, j=j: e.transpose(
            psB[j][:, :], scw[j][:, :], sidf[:, :]), wait=[cw, te2_done.get(i - 2)])
        tb_done[i] = tb
        te2 = p.emit("vector", lambda e, j=j: e.tensor_copy(
            out=scT[j][:, :], in_=psB[j][:, :]), wait=[tb, cw_dma.get(i - 2)])
        te2_done[i] = te2
        cw_dma[i] = p.emit("sync", lambda e, i=i, j=j: e.dma_start(
            out=apcw[0:8, i * 128:(i + 1) * 128], in_=scT[j][:, :]),
            wait=[te2], dma=True, dsem=f"co{j}")
    st.drain()
    return st.done()


@stage("moe")
def build_moe():
    T = S // 2
    st = Stage("moe")
    p = st.p
    apxT = st.inp("xnT_bf", (D, T), BF16)
    apcw = st.inp("cw_row", (1, T))
    apacc = st.inp("acc_in", (D, T))
    apgu = st.inp("wgu", (32, 128, 2048))
    apwd = st.inp("wd", (8, 128, 4096))
    apones = st.inp("ones_row", (1, 128))
    apout = st.outp("acc_out", (D, T))
    ximg = st.sb("ximg", [128, NC * T], BF16)
    aimg = st.sb("aimg", [128, NC * T], F32)
    himg = st.sb("himg", [128, 32 * T], BF16)
    scw = st.sb("scw", [1, T], F32)
    sone = st.sb("sone", [1, 128], F32)
    scwb = st.sb("scwb", [128, T], F32)
    ssil = [st.sb(f"ssil{j}", [128, 512], F32) for j in range(2)]
    stmp = [st.sb(f"stmp{j}", [128, 512], F32) for j in range(2)]
    psC = st.ps("psC", [128, 512])
    psGU = [st.ps(f"psGU{j}", [128, 512]) for j in range(4)]
    psD = [st.ps(f"psD{j}", [128, 512]) for j in range(2)]

    xt = dma_img_in(st, ximg, apxT, D, T, dsem="xi")
    at = dma_img_in(st, aimg, apacc, D, T, dsem="ai")
    c1 = p.emit("sync", lambda e: e.dma_start(out=scw[:, :], in_=apcw[:, :]),
                dma=True, dsem=st.kc())
    c2 = p.emit("sync", lambda e: e.dma_start(out=sone[:, :], in_=apones[:, :]),
                dma=True, dsem=st.kc())
    cb_tok = None
    for t in range(T // 512):
        tm = p.emit("tensor", lambda e, t=t: e.matmul(
            psC[:, :], sone[:, :], scw[0:1, t * 512:(t + 1) * 512],
            start=True, stop=True), wait=[c1, c2, cb_tok])
        cb_tok = p.emit("vector", lambda e, t=t: e.tensor_copy(
            out=scwb[:, t * 512:(t + 1) * 512], in_=psC[:, :]), wait=[tm])

    wgu = WStream(st, "wgu", 2048, [apgu[f, :, :] for f in range(32)])
    mq = MMQ(st, psGU)
    h_tok = None
    for f in range(32):
        sil_tok = {}
        for th in range(2):
            gp = [(wgu.buf(f)[:, k * 128:(k + 1) * 128],
                   ximg[:, k * T + th * 512:k * T + th * 512 + 512], None)
                  for k in range(NC)]
            gt = mq.group(gp, [("scalar",
                                lambda e, b, th=th: e.activation(
                                    out=ssil[th][:, :], in_=b[:, :], func=AF.Silu),
                                [h_tok])],
                          first_wait=[wgu.ready(f), xt])
            sil_tok[th] = gt
            up = [(wgu.buf(f)[:, 1024 + k * 128:1024 + (k + 1) * 128],
                   ximg[:, k * T + th * 512:k * T + th * 512 + 512], None)
                  for k in range(NC)]
            h_tok = mq.group(up, [("vector",
                                   lambda e, b, f=f, th=th: e.tensor_tensor(
                                       out=himg[:, f * T + th * 512:f * T + th * 512 + 512],
                                       in0=ssil[th][:, :], in1=b[:, :], op=ALU.mult),
                                   [sil_tok[th]])],
                          first_wait=[])
        wgu.release(f, h_tok)

    wd = WStream(st, "wd", 4096, [apwd[o, :, :] for o in range(8)])
    md = MMQ(st, psD)
    acc_tok = {}
    for o in range(8):
        last = None
        for th in range(2):
            dp = [(wd.buf(o)[:, f * 128:(f + 1) * 128],
                   himg[:, f * T + th * 512:f * T + th * 512 + 512], None)
                  for f in range(32)]

            def evac1(e, b, th=th):
                return e.tensor_tensor(out=stmp[th][:, :], in0=b[:, :],
                                       in1=scwb[:, th * 512:th * 512 + 512],
                                       op=ALU.mult)

            def evac2(e, b, o=o, th=th):
                sl = slice(o * T + th * 512, o * T + th * 512 + 512)
                return e.tensor_add(out=aimg[:, sl], in0=aimg[:, sl],
                                    in1=stmp[th][:, :])
            last = md.group(dp, [("vector", evac1, [cb_tok, acc_tok.get((o - 1, th))]),
                                 ("vector", evac2, [at])],
                            first_wait=[wd.ready(o), h_tok])
            acc_tok[(o, th)] = last
        wd.release(o, last)
        p.emit("sync", lambda e, o=o: e.dma_start(
            out=apout[o * 128:(o + 1) * 128, :], in_=aimg[:, o * T:(o + 1) * T]),
            wait=[last], dma=True, dsem="ao")
    st.drain()
    return st.done()


@stage("combine")
def build_combine():
    st = Stage("combine")
    p = st.p
    apx = st.inp("x", (S, D))
    apsT = st.inp("sT", (D, S), BF16)
    apacc = st.inp("acc", (D, S))
    apid = st.inp("id_bf", (128, 128), BF16)
    apo = st.outp("xo", (S, D))
    sid = st.sb("sid", [128, 128], BF16)
    sacc = [st.sb(f"sacc{j}", [128, S], F32) for j in range(2)]
    sgb = [st.sb(f"sgb{j}", [128, S], BF16) for j in range(2)]
    ssb = [st.sb(f"ssb{j}", [128, S], BF16) for j in range(2)]
    gtok = st.sb("gtok", [128, NT * D], BF16)
    stok = st.sb("stok", [128, NT * D], BF16)
    sx = [st.sb(f"sx{j}", [128, D], F32) for j in range(2)]
    sm = [st.sb(f"sm{j}", [128, D], F32) for j in range(2)]
    pst = [st.ps(f"pst{j}", [128, 512], BF16) for j in range(2)]

    idt = p.emit("sync", lambda e: e.dma_start(out=sid[:, :], in_=apid[:, :]),
                 dma=True, dsem=st.kc())
    gidx = 0
    ps_free = {}
    sv_done, g_tr_done, s_tr_done = {}, {}, {}
    last_tr = None
    for c in range(NC):
        j = c % 2
        at = p.emit("sync", lambda e, c=c, j=j: e.dma_start(
            out=sacc[j][:, :], in_=apacc[c * 128:(c + 1) * 128, :]),
            wait=[sv_done.get(c - 2)], dma=True, dsem=f"sa{j}")
        sv = p.emit("scalar", lambda e, j=j: e.activation(
            out=sgb[j][:, :], in_=sacc[j][:, :], func=AF.Silu),
            wait=[at, g_tr_done.get(c - 2)])
        sv_done[c] = sv
        bt = p.emit("sync", lambda e, c=c, j=j: e.dma_start(
            out=ssb[j][:, :], in_=apsT[c * 128:(c + 1) * 128, :]),
            wait=[s_tr_done.get(c - 2)], dma=True, dsem=f"sb{j}")
        for src, dst, rdy in ((sgb, gtok, sv), (ssb, stok, bt)):
            for i4 in range(4):
                tp = None
                for k in range(4):
                    i = i4 * 4 + k
                    tp = p.emit("tensor", lambda e, j=j, i=i, k=k, g=gidx % 2,
                                src=src: e.transpose(
                        pst[g][:, k * 128:(k + 1) * 128],
                        src[j][:, i * 128:(i + 1) * 128], sid[:, :]),
                        wait=([rdy, idt, ps_free.get(gidx - 2)] if k == 0 else []))
                dst_ap = dst[:, :].rearrange("p (i f) -> p i f", f=D)[
                    :, i4 * 4:i4 * 4 + 4, c * 128:(c + 1) * 128]
                last_tr = p.emit("vector", lambda e, g=gidx % 2, d=dst_ap: e.tensor_copy(
                    out=d, in_=pst[g][:, :].rearrange("p (i f) -> p i f", f=128)),
                    wait=[tp])
                ps_free[gidx] = last_tr
                if src is sgb:
                    g_tr_done[c] = tp
                else:
                    s_tr_done[c] = tp
                gidx += 1
    out_done = {}
    for i in range(NT):
        j = i % 2
        xt = p.emit("sync", lambda e, i=i, j=j: e.dma_start(
            out=sx[j][:, :], in_=apx[i * 128:(i + 1) * 128, :]),
            wait=[out_done.get(i - 2)], dma=True, dsem=f"sx{j}")
        mt = p.emit("vector", lambda e, i=i, j=j: e.tensor_tensor(
            out=sm[j][:, :], in0=stok[:, i * D:(i + 1) * D],
            in1=gtok[:, i * D:(i + 1) * D], op=ALU.mult),
            wait=[last_tr, out_done.get(i - 2)])
        ot = p.emit("vector", lambda e, i=i, j=j: e.tensor_add(
            out=sm[j][:, :], in0=sm[j][:, :], in1=sx[j][:, :]), wait=[xt])
        out_done[i] = p.emit("sync", lambda e, i=i, j=j: e.dma_start(
            out=apo[i * 128:(i + 1) * 128, :], in_=sm[j][:, :]), wait=[ot],
            dma=True, dsem=f"so{j}")
    st.drain()
    return st.done()


DEBUG_L0 = False


@stage("l0attn")
def build_l0attn():
    st = Stage("l0attn")
    p = st.p
    apxT = st.inp("xnT_bf", (D, S), BF16)
    apw = {w: st.inp("w" + w, (2, 128, 8 * 512)) for w in ("q", "k", "v")}
    apwo = st.inp("wo", (8, 128, 1024))
    aptab = {t: st.inp(t, (128, 16 * 32)) for t in
             ("r0_cos_q", "r0_sin_q", "r0_cos_k", "r0_sin_k")}
    apid = st.inp("id_bf", (128, 128), BF16)
    apml = st.inp("mask_low", (128, 128))
    apsT = st.outp("sT", (D, S), BF16)

    ximg = st.sb("ximg", [128, NC * S], BF16)
    qT = st.sb("qT", [128, NC * S], BF16)
    kT = st.sb("kT", [128, NC * S], BF16)
    vtok = st.sb("vtok", [128, NT * D], BF16)
    aT = ximg
    stab = {t: st.sb("s" + t, [128, 16 * 32], F32) for t in aptab}
    sid = st.sb("sid", [128, 128], BF16)
    sml = st.sb("sml", [128, 128], F32)
    sra = [st.sb(f"sra{j}", [128, 256], F32) for j in range(2)]
    srb = [st.sb(f"srb{j}", [128, 256], F32) for j in range(2)]
    qrot = [st.sb(f"qrot{j}", [128, 512], BF16) for j in range(2)]
    sPex = st.sb("sPex", [128, S], BF16)
    sPT = st.sb("sPT", [128, S], BF16)
    srv = st.sb("srv", [128, 8], F32)
    stgs = [st.sb(f"stgs{j}", [128, 512], BF16) for j in range(2)]
    ps_sc = st.ps("ps_sc", [128, 2048])
    ps_t = [st.ps(f"ps_t{j}", [128, 1024], BF16) for j in range(2)]
    ps_av = [st.ps(f"ps_av{j}", [64, 512]) for j in range(2)]

    xt = dma_img_in(st, ximg, apxT, D, S, dsem="xi")
    tid = p.emit("sync", lambda e: e.dma_start(out=sid[:, :], in_=apid[:, :]),
                 dma=True, dsem=st.kc())
    tml = p.emit("sync", lambda e: e.dma_start(out=sml[:, :], in_=apml[:, :]),
                 dma=True, dsem=st.kc())
    ttab = {}
    for t in aptab:
        ttab[t] = p.emit("sync", lambda e, t=t: e.dma_start(
            out=stab[t][:, :], in_=aptab[t][:, :]), dma=True, dsem=st.kc())

    units = [apw[w][n, :, :] for w in ("q", "k", "v") for n in range(2)]
    ws = WStream(st, "wqkv", 8 * 512, units)
    rope_free = {}
    sc_banks = [ps_sc[:, j * 512:(j + 1) * 512] for j in range(4)]
    mq = MMQ(st, sc_banks)
    tp_par = 0
    tp_free = {}
    qkv_end = []

    for wi, w in enumerate(("q", "k", "v")):
        for n in range(2):
            u = wi * 2 + n
            for i in range(NT):
                pairs = [(ximg[:, k * S + i * 128:k * S + i * 128 + 128],
                          ws.buf(u)[:, k * 512:(k + 1) * 512], None)
                         for k in range(NC)]
                if w == "v":
                    def ev(e, b, i=i, n=n):
                        return e.copy(out=vtok[:, i * D + n * 512:i * D + n * 512 + 512],
                                      in_=b[:, :])
                    tok = mq.group(pairs, [("scalar", ev, [])],
                                   first_wait=[ws.ready(u), xt])
                    qkv_end.append(tok)
                else:
                    ct = "r0_cos_q" if w == "q" else "r0_cos_k"
                    sn = "r0_sin_q" if w == "q" else "r0_sin_k"
                    par = i % 2
                    cosb = stab[ct][:, i * 32:(i + 1) * 32].unsqueeze(1) \
                        .broadcast_to([128, 8, 32])
                    sinb = stab[sn][:, i * 32:(i + 1) * 32].unsqueeze(1) \
                        .broadcast_to([128, 8, 32])
                    r3 = lambda a: a.rearrange("p (h d) -> p h d", d=32)

                    def mk(par, cosb, sinb, r3):
                        def f1(e, b):
                            return e.tensor_tensor(out=r3(sra[par][:, :]),
                                                   in0=r3(b[:, 0:512:2]), in1=cosb, op=ALU.mult)

                        def f2(e, b):
                            return e.tensor_tensor(out=r3(srb[par][:, :]),
                                                   in0=r3(b[:, 1:512:2]), in1=sinb, op=ALU.mult)

                        def f3(e, b):
                            return e.tensor_sub(out=qrot[par][:, 0:512:2],
                                                in0=sra[par][:, :], in1=srb[par][:, :])

                        def f4(e, b):
                            return e.tensor_tensor(out=r3(sra[par][:, :]),
                                                   in0=r3(b[:, 0:512:2]), in1=sinb, op=ALU.mult)

                        def f5(e, b):
                            return e.tensor_tensor(out=r3(srb[par][:, :]),
                                                   in0=r3(b[:, 1:512:2]), in1=cosb, op=ALU.mult)

                        def f6(e, b):
                            return e.tensor_add(out=qrot[par][:, 1:512:2],
                                                in0=sra[par][:, :], in1=srb[par][:, :])
                        return [f1, f2, f3, f4, f5, f6]
                    fs = mk(par, cosb, sinb, r3)
                    rope_w = [ttab[ct], ttab[sn], rope_free.get((w, i - 2))] \
                        if True else []
                    tok = mq.group(pairs,
                                   [("vector", f, (rope_w if ix == 0 else []))
                                    for ix, f in enumerate(fs)],
                                   first_wait=[ws.ready(u), xt])
                    rope_free[(w, i)] = tok
                    dst = qT if w == "q" else kT
                    tp = None
                    for k in range(4):
                        tp = p.emit("tensor", lambda e, par=par, k=k, tpp=tp_par % 2:
                                    e.transpose(ps_t[tpp][:, k * 128:(k + 1) * 128],
                                                qrot[par][:, k * 128:(k + 1) * 128],
                                                sid[:, :]),
                                    wait=([tok, tid, tp_free.get(tp_par - 2)]
                                          if k == 0 else []))
                    dap = dst[:, :].rearrange("p (c t) -> p c t", t=S)[
                        :, n * 4:n * 4 + 4, i * 128:(i + 1) * 128]
                    tev = p.emit("vector", lambda e, tpp=tp_par % 2, dap=dap:
                                 e.tensor_copy(out=dap, in_=ps_t[tpp][:, 0:512]
                                               .rearrange("p (c t) -> p c t", t=128)),
                                 wait=[tp])
                    tp_free[tp_par] = tev
                    tp_par += 1
                    qkv_end.append(tev)
            ws.release(u, qkv_end[-1])

    attn_done = []
    qkv_wait = {}
    for t in qkv_end:
        if t:
            qkv_wait[t[0]] = max(qkv_wait.get(t[0], 0), t[1])
    qkv_wait = list(qkv_wait.items())
    sc_free = list(mq.free_tok)
    exp_prev = None
    av_free = {}
    hq = 0
    for h in range(16):
        base = (h // 2) * S
        row = (h % 2) * 64
        for qt in range(NT):
            L = 128 * (qt + 1)
            nk = (L + 511) // 512
            par = hq % 2
            mm_tok = None
            for jx in range(nk):
                wdt = min(512, L - jx * 512)
                wfirst = ([exp_prev] + qkv_wait + sc_free) if jx == 0 else []
                mm_tok = p.emit("tensor", lambda e, h=h, qt=qt, jx=jx, wdt=wdt,
                                base=base, row=row:
                                e.matmul(ps_sc[:, jx * 512:jx * 512 + wdt],
                                         qT[row:row + 64, base + qt * 128:base + qt * 128 + 128],
                                         kT[row:row + 64, base + jx * 512:base + jx * 512 + wdt],
                                         start=True, stop=True),
                                wait=wfirst)
            sc_free = []
            msk = p.emit("vector", lambda e, qt=qt: e.tensor_tensor(
                out=ps_sc[:, qt * 128:qt * 128 + 128],
                in0=ps_sc[:, qt * 128:qt * 128 + 128], in1=sml[:, :], op=ALU.add),
                wait=[mm_tok, tml])
            nmx = p.emit("vector", lambda e, L=L, par=par: e.reduce_max(
                out=srv[:, 4 * par:4 * par + 1], in_=ps_sc[:, 0:L], axis=AX.X,
                negate=True), wait=[msk])
            expt = p.emit("scalar", lambda e, L=L, par=par: e.activation(
                out=sPex[:, 0:L], in_=ps_sc[:, 0:L], func=AF.Exp,
                bias=srv[:, 4 * par:4 * par + 1],
                accum_out=srv[:, 4 * par + 1:4 * par + 2]), wait=[nmx])
            exp_prev = expt
            rcp = p.emit("vector", lambda e, par=par: e.reciprocal(
                out=srv[:, 4 * par + 2:4 * par + 3],
                in_=srv[:, 4 * par + 1:4 * par + 2]), wait=[expt])
            scl = p.emit("vector", lambda e, L=L, par=par: e.tensor_scalar(
                out=sPex[:, 0:L], in0=sPex[:, 0:L],
                scalar1=srv[:, 4 * par + 2:4 * par + 3], scalar2=None,
                op0=ALU.mult), wait=[rcp])
            ev_tok = None
            for g4 in range((qt + 1 + 3) // 4):
                kts = range(g4 * 4, min(qt + 1, g4 * 4 + 4))
                tp = None
                for ix, kt in enumerate(kts):
                    tp = p.emit("tensor", lambda e, kt=kt, ix=ix, tpp=tp_par % 2:
                                e.transpose(ps_t[tpp][:, ix * 128:(ix + 1) * 128],
                                            sPex[:, kt * 128:(kt + 1) * 128], sid[:, :]),
                                wait=([scl, tp_free.get(tp_par - 2)] if ix == 0 else []))
                nn = len(list(kts))
                ev_tok = p.emit("vector", lambda e, g4=g4, nn=nn, tpp=tp_par % 2:
                                e.tensor_copy(out=sPT[:, g4 * 512:g4 * 512 + nn * 128],
                                              in_=ps_t[tpp][:, 0:nn * 128]),
                                wait=[tp])
                tp_free[tp_par] = ev_tok
                tp_par += 1
            av_tok = None
            for kt in range(qt + 1):
                wfirst = [ev_tok, av_free.get(hq - 2)] if kt == 0 else []
                av_tok = p.emit("tensor", lambda e, kt=kt, h=h, par=par, qt=qt:
                                e.matmul(ps_av[par][0:64, 0:128],
                                         vtok[:, kt * D + h * 64:kt * D + h * 64 + 64],
                                         sPT[:, kt * 128:(kt + 1) * 128],
                                         start=(kt == 0), stop=(kt == qt)),
                                wait=wfirst)
            at = p.emit("scalar", lambda e, par=par, row=row, base=base, qt=qt:
                        e.copy(out=aT[row:row + 64, base + qt * 128:base + qt * 128 + 128],
                               in_=ps_av[par][0:64, 0:128]), wait=[av_tok])
            av_free[hq] = at
            attn_done.append(at)
            hq += 1

    wo = WStream(st, "wo", 1024, [apwo[f, :, :] for f in range(8)])
    mq2 = MMQ(st, sc_banks)
    mq2.free_tok = [exp_prev] * 4
    stg_free = {}
    gg = 0
    for f in range(8):
        last = None
        for t4 in range(4):
            pairs = [(wo.buf(f)[:, k * 128:(k + 1) * 128],
                      aT[:, k * S + t4 * 512:k * S + t4 * 512 + 512], None)
                     for k in range(NC)]

            def evs(e, b, par=gg % 2):
                return e.copy(out=stgs[par][:, :], in_=b[:, :])
            tok = mq2.group(pairs, [("scalar", evs,
                                     [stg_free.get(gg - 2), attn_done[-1]])],
                            first_wait=[wo.ready(f), attn_done[-1]])
            dt_ = p.emit("sync", lambda e, f=f, t4=t4, par=gg % 2: e.dma_start(
                out=apsT[f * 128:(f + 1) * 128, t4 * 512:(t4 + 1) * 512],
                in_=stgs[par][:, :]), wait=[tok], dma=True, dsem=f"so{gg % 2}")
            stg_free[gg] = dt_
            gg += 1
            last = tok
        wo.release(f, last)
    if DEBUG_L0:
        wda = [attn_done[-1], exp_prev]
        dma_img_out(st, st.outp("dbg_qT", (D, S), BF16), qT, D, S, wait=wda)
        dma_img_out(st, st.outp("dbg_kT", (D, S), BF16), kT, D, S, wait=wda)
        dma_img_out(st, st.outp("dbg_vtok", (S, D), BF16), vtok, S, D, wait=wda)
        dma_img_out(st, st.outp("dbg_aT", (D, S), BF16), aT, D, S, wait=wda)
    st.drain()
    return st.done()


DEBUG_IN = False


@stage("inattn")
def build_inner_attn():
    st = Stage("inattn")
    p = st.p
    apx = st.inp("x_in", (D, S))
    apwqkv = st.inp("wqkv", (6, 128, 8 * 512))
    apwo = st.inp("wo", (8, 128, 1024))
    apqs = st.inp("qs64", (1, 64))
    apqsr = st.inp("qs64r", (1, 64))
    apks = st.inp("ks64", (1, 64))
    apksr = st.inp("ks64r", (1, 64))
    aptab = {t: st.inp(t, (128, 16 * 64)) for t in
             ("r1_cs_q", "r1_sn_q", "r1_cs_k", "r1_sn_k")}
    apid = st.inp("id_bf", (128, 128), BF16)
    apml = st.inp("mask_low", (128, 128))
    apmu = st.inp("mask_upp", (128, 128))
    apones = st.inp("ones_row", (1, 128))
    apxo = st.outp("x_out", (D, S))

    xs_bf = st.sb("xs_bf", [128, NC * S], BF16)
    qT = st.sb("qT", [128, NC * S], BF16)
    kT = st.sb("kT", [128, NC * S], BF16)
    vtok = st.sb("vtok", [128, NT * D], BF16)
    aT = xs_bf
    stab = {t: st.sb("s" + t, [128, 16 * 64], F32) for t in aptab}
    sid = st.sb("sid", [128, 128], BF16)
    sml = st.sb("sml", [128, 128], F32)
    smu_ = st.sb("smu_", [128, 128], F32)
    sone = st.sb("sone", [1, 128], F32)
    sqs = st.sb("sqs", [1, 256], F32)
    sqsb = st.sb("sqsb", [128, 64], F32)
    sxc = [st.sb(f"sxc{j}", [128, S], F32) for j in range(2)]
    sxr = sxc
    snrm = st.sb("snrm", [128, 16], F32)
    sq1_ = st.sb("sq1", [128, 512], F32)
    sq1 = [sq1_, sq1_]
    sra_ = st.sb("sra", [128, 256], F32)
    sra = [sra_, sra_]
    srb_ = st.sb("srb", [128, 256], F32)
    srb = [srb_, srb_]
    qrot = [st.sb(f"qrot{j}", [128, 512], BF16) for j in range(2)]
    sPex = st.sb("sPex", [128, 512], BF16)
    sPT = st.sb("sPT", [128, 512], BF16)
    srv = st.sb("srv", [128, 8], F32)
    stgo = [st.sb(f"stgo{j}", [128, 512], F32) for j in range(2)]
    ssq = stgo
    ps_sc = st.ps("ps_sc", [128, 2048])
    ps_t = [st.ps(f"ps_t{j}", [128, 1024], BF16) for j in range(2)]
    ps_av = [st.ps(f"ps_av{j}", [64, 512]) for j in range(2)]

    tid = p.emit("sync", lambda e: e.dma_start(out=sid[:, :], in_=apid[:, :]),
                 dma=True, dsem=st.kc())
    tml = p.emit("sync", lambda e: e.dma_start(out=sml[:, :], in_=apml[:, :]),
                 dma=True, dsem=st.kc())
    tmu = p.emit("sync", lambda e: e.dma_start(out=smu_[:, :], in_=apmu[:, :]),
                 dma=True, dsem=st.kc())
    ton = p.emit("sync", lambda e: e.dma_start(out=sone[:, :], in_=apones[:, :]),
                 dma=True, dsem=st.kc())
    tq = []
    for ix, a in enumerate((apqs, apqsr, apks, apksr)):
        tq.append(p.emit("sync", lambda e, ix=ix, a=a: e.dma_start(
            out=sqs[0:1, ix * 64:(ix + 1) * 64], in_=a[:, :]), dma=True,
            dsem=st.kc()))
    ttab = {}
    for t in aptab:
        ttab[t] = p.emit("sync", lambda e, t=t: e.dma_start(
            out=stab[t][:, :], in_=aptab[t][:, :]), dma=True, dsem=st.kc())
    prim = {}
    cp = None
    for ix, (t, qcol) in enumerate([("r1_cs_q", 0), ("r1_sn_q", 1),
                                    ("r1_cs_k", 2), ("r1_sn_k", 3)]):
        bc = p.emit("tensor", lambda e, qcol=qcol: e.matmul(
            ps_sc[:, 0:64], sone[:, :], sqs[0:1, qcol * 64:(qcol + 1) * 64],
            start=True, stop=True), wait=[ton, tq[qcol], prim.get(ix - 1)])
        cp = p.emit("vector", lambda e: e.tensor_copy(
            out=sqsb[:, :], in_=ps_sc[:, 0:64]), wait=[bc])
        prim[ix] = p.emit("vector", lambda e, t=t: e.tensor_tensor(
            out=stab[t][:, :].rearrange("p (i d) -> p i d", d=64),
            in0=stab[t][:, :].rearrange("p (i d) -> p i d", d=64),
            in1=sqsb[:, :].unsqueeze(1).broadcast_to([128, 16, 64]),
            op=ALU.mult), wait=[cp, ttab[t]])
    tab_ok = prim[3]

    xs_done = []
    for c in range(NC):
        j = c % 2
        xt = p.emit("sync", lambda e, c=c, j=j: e.dma_start(
            out=sxc[j][:, :], in_=apx[c * 128:(c + 1) * 128, :]),
            wait=[xs_done[-2] if len(xs_done) >= 2 else None], dma=True,
            dsem=f"sx{j}")
        if c < 4:
            ct = p.emit("scalar", lambda e, c=c, j=j: e.copy(
                out=xs_bf[:, c * S:(c + 1) * S], in_=sxc[j][:, :]), wait=[xt])
        else:
            ct = p.emit("scalar", lambda e, c=c, j=j: e.copy(
                out=xs_bf[:, c * S + 1:(c + 1) * S], in_=sxc[j][:, 0:S - 1]),
                wait=[xt])
            p.emit("vector", lambda e, c=c: e.memset(
                xs_bf[:, c * S:c * S + 1], 0.0), wait=[])
        xs_done.append(ct)
    xs_wait = [xs_done[-1], xs_done[-2], p.now("vector")]

    units = [apwqkv[n, :, :] for n in range(6)]
    ws = WStream(st, "wqkv", 8 * 512, units, nbuf=1)
    sc_banks = [ps_sc[:, j * 512:(j + 1) * 512] for j in range(4)]
    mq = MMQ(st, sc_banks)
    mq.free_tok[0] = cp
    tp_par = 0
    tp_free = {}
    tf_free = {}
    qkv_end = []
    for n in range(6):
        w = ("q", "q", "k", "k", "v", "v")[n]
        for i in range(NT):
            pairs = [(xs_bf[:, k * S + i * 128:k * S + i * 128 + 128],
                      ws.buf(n)[:, k * 512:(k + 1) * 512], None)
                     for k in range(NC)]
            if w == "v":
                def ev(e, b, i=i, n=n):
                    return e.copy(out=vtok[:, i * D + (n - 4) * 512:
                                           i * D + (n - 4) * 512 + 512], in_=b[:, :])
                tok = mq.group(pairs, [("scalar", ev, [])],
                               first_wait=[ws.ready(n)] + xs_wait)
                qkv_end.append(tok)
            else:
                par = i % 2
                csn = "r1_cs_q" if w == "q" else "r1_cs_k"
                snn = "r1_sn_q" if w == "q" else "r1_sn_k"
                nh = 8
                r64 = lambda a: a.rearrange("p (h d) -> p h d", d=64)
                r32l = lambda a: a.rearrange("p (h d) -> p h d", d=64)[:, :, 0:32]
                r32h = lambda a: a.rearrange("p (h d) -> p h d", d=64)[:, :, 32:64]
                CSl = stab[csn][:, i * 64:i * 64 + 32].unsqueeze(1) \
                    .broadcast_to([128, 8, 32])
                CSh = stab[csn][:, i * 64 + 32:(i + 1) * 64].unsqueeze(1) \
                    .broadcast_to([128, 8, 32])
                SNl = stab[snn][:, i * 64:i * 64 + 32].unsqueeze(1) \
                    .broadcast_to([128, 8, 32])
                SNh = stab[snn][:, i * 64 + 32:(i + 1) * 64].unsqueeze(1) \
                    .broadcast_to([128, 8, 32])

                def mk(par, CSl, CSh, SNl, SNh):
                    def a1(e, b):
                        return e.activation(out=ssq[par][:, :], in_=b[:, :],
                                            func=AF.Square)

                    def a2(e, b):
                        return e.reduce_sum(out=snrm[:, 8 * par:8 * par + 8],
                                            in_=r64(ssq[par][:, :]), axis=AX.X)

                    def a3(e, b):
                        return e.tensor_scalar_max(
                            out=snrm[:, 8 * par:8 * par + 8],
                            in0=snrm[:, 8 * par:8 * par + 8], scalar1=1e-24)

                    def a4(e, b):
                        return e.activation(out=snrm[:, 8 * par:8 * par + 8],
                                            in_=snrm[:, 8 * par:8 * par + 8],
                                            func=AF.Sqrt)

                    def a5(e, b):
                        return e.reciprocal(out=snrm[:, 8 * par:8 * par + 8],
                                            in_=snrm[:, 8 * par:8 * par + 8])

                    def a6(e, b):
                        return e.tensor_tensor(
                            out=r64(sq1[par][:, :]), in0=r64(b[:, :]),
                            in1=snrm[:, 8 * par:8 * par + 8].unsqueeze(2)
                            .broadcast_to([128, 8, 64]), op=ALU.mult)

                    r32c = lambda a: a.rearrange("p (h d) -> p h d", d=32)

                    def a7(e, b):
                        return e.tensor_tensor(out=r32c(sra[par][:, :]),
                                               in0=r32l(sq1[par][:, :]), in1=CSl,
                                               op=ALU.mult)

                    def a8(e, b):
                        return e.tensor_tensor(out=r32c(srb[par][:, :]),
                                               in0=r32h(sq1[par][:, :]), in1=SNl,
                                               op=ALU.mult)

                    def a9(e, b):
                        return e.tensor_sub(out=r32l(qrot[par][:, :]),
                                            in0=r32c(sra[par][:, :]),
                                            in1=r32c(srb[par][:, :]))

                    def aA(e, b):
                        return e.tensor_tensor(out=r32c(sra[par][:, :]),
                                               in0=r32h(sq1[par][:, :]), in1=CSh,
                                               op=ALU.mult)

                    def aB(e, b):
                        return e.tensor_tensor(out=r32c(srb[par][:, :]),
                                               in0=r32l(sq1[par][:, :]), in1=SNh,
                                               op=ALU.mult)

                    def aC(e, b):
                        return e.tensor_add(out=r32h(qrot[par][:, :]),
                                            in0=r32c(sra[par][:, :]),
                                            in1=r32c(srb[par][:, :]))
                    return [("scalar", a1), ("vector", a2), ("vector", a3),
                            ("scalar", a4), ("vector", a5), ("vector", a6),
                            ("vector", a7), ("vector", a8), ("vector", a9),
                            ("vector", aA), ("vector", aB), ("vector", aC)]
                fs = mk(par, CSl, CSh, SNl, SNh)
                evl = [(eng, f, ([tab_ok, tf_free.get((w, i - 2))] if ix == 0 else []))
                       for ix, (eng, f) in enumerate(fs)]
                tok = mq.group(pairs, evl, first_wait=[ws.ready(n)] + xs_wait)
                tf_free[(w, i)] = tok
                dst = qT if w == "q" else kT
                tp = None
                for k in range(4):
                    tp = p.emit("tensor", lambda e, par=par, k=k, tpp=tp_par % 2:
                                e.transpose(ps_t[tpp][:, k * 128:(k + 1) * 128],
                                            qrot[par][:, k * 128:(k + 1) * 128],
                                            sid[:, :]),
                                wait=([tok, tid, tp_free.get(tp_par - 2)]
                                      if k == 0 else []))
                dap = dst[:, :].rearrange("p (c t) -> p c t", t=S)[
                    :, (n % 2) * 4:(n % 2) * 4 + 4, i * 128:(i + 1) * 128]
                tev = p.emit("vector", lambda e, tpp=tp_par % 2, dap=dap:
                             e.tensor_copy(out=dap, in_=ps_t[tpp][:, 0:512]
                                           .rearrange("p (c t) -> p c t", t=128)),
                             wait=[tp])
                tp_free[tp_par] = tev
                tp_par += 1
                qkv_end.append(tev)
        ws.release(n, qkv_end[-1])

    qkv_wait = {}
    for t in qkv_end:
        qkv_wait[t[0]] = max(qkv_wait.get(t[0], 0), t[1])
    qkv_wait = list(qkv_wait.items())
    attn_done = []
    sc_used = {}
    av_free = {}
    exp_done = {}
    idx = 0
    for h in range(16):
        base = (h // 2) * S
        row = (h % 2) * 64
        for wb in range(8):
            for qs_ in range(2):
                q0 = wb * 256 + qs_ * 128
                kstart = max(0, q0 - 256)
                span = (q0 + 128 - kstart) // 128
                L = span * 128
                par4 = idx % 4
                scs = ps_sc[:, par4 * 512:par4 * 512 + L]
                wf = [exp_done.get(idx - 4)] + (qkv_wait if idx == 0 else [])
                mt = p.emit("tensor", lambda e, row=row, base=base, q0=q0,
                            kstart=kstart, L=L, scs=scs:
                            e.matmul(scs[:, 0:L],
                                     qT[row:row + 64, base + q0:base + q0 + 128],
                                     kT[row:row + 64, base + kstart:base + kstart + L],
                                     start=True, stop=True), wait=wf)
                mk1 = p.emit("vector", lambda e, scs=scs, L=L: e.tensor_tensor(
                    out=scs[:, L - 128:L], in0=scs[:, L - 128:L], in1=sml[:, :],
                    op=ALU.add), wait=[mt, tml])
                if span == 3:
                    mk1 = p.emit("vector", lambda e, scs=scs: e.tensor_tensor(
                        out=scs[:, 0:128], in0=scs[:, 0:128], in1=smu_[:, :],
                        op=ALU.add), wait=[tmu])
                par = idx % 2
                nmx = p.emit("vector", lambda e, scs=scs, L=L, par=par:
                             e.reduce_max(out=srv[:, 4 * par:4 * par + 1],
                                          in_=scs[:, 0:L], axis=AX.X, negate=True),
                             wait=[mk1])
                ext = p.emit("scalar", lambda e, scs=scs, L=L, par=par:
                             e.activation(out=sPex[:, 0:L], in_=scs[:, 0:L],
                                          func=AF.Exp, bias=srv[:, 4 * par:4 * par + 1],
                                          accum_out=srv[:, 4 * par + 1:4 * par + 2]),
                             wait=[nmx, exp_done.get(idx - 1)])
                exp_done[idx] = ext
                rcp = p.emit("vector", lambda e, par=par: e.reciprocal(
                    out=srv[:, 4 * par + 2:4 * par + 3],
                    in_=srv[:, 4 * par + 1:4 * par + 2]), wait=[ext])
                scl = p.emit("vector", lambda e, L=L, par=par: e.tensor_scalar(
                    out=sPex[:, 0:L], in0=sPex[:, 0:L],
                    scalar1=srv[:, 4 * par + 2:4 * par + 3], scalar2=None,
                    op0=ALU.mult), wait=[rcp])
                tp = None
                for kt in range(span):
                    tp = p.emit("tensor", lambda e, kt=kt, tpp=tp_par % 2:
                                e.transpose(ps_t[tpp][:, kt * 128:(kt + 1) * 128],
                                            sPex[:, kt * 128:(kt + 1) * 128],
                                            sid[:, :]),
                                wait=([scl, tp_free.get(tp_par - 2)] if kt == 0
                                      else []))
                ev = p.emit("vector", lambda e, span=span, tpp=tp_par % 2:
                            e.tensor_copy(out=sPT[:, 0:span * 128],
                                          in_=ps_t[tpp][:, 0:span * 128]),
                            wait=[tp])
                tp_free[tp_par] = ev
                tp_par += 1
                avt = None
                for kt in range(span):
                    ka = kstart // 128 + kt
                    wf2 = [ev, av_free.get(idx - 2)] if kt == 0 else []
                    avt = p.emit("tensor", lambda e, ka=ka, kt=kt, h=h, par=par,
                                 span=span:
                                 e.matmul(ps_av[par][0:64, 0:128],
                                          vtok[:, ka * D + h * 64:ka * D + h * 64 + 64],
                                          sPT[:, kt * 128:(kt + 1) * 128],
                                          start=(kt == 0), stop=(kt == span - 1)),
                                 wait=wf2)
                at = p.emit("scalar", lambda e, par=par, row=row, base=base, q0=q0:
                            e.copy(out=aT[row:row + 64, base + q0:base + q0 + 128],
                                   in_=ps_av[par][0:64, 0:128]), wait=[avt])
                av_free[idx] = at
                attn_done.append(at)
                idx += 1

    wo = WStream(st, "wo", 1024, [apwo[f, :, :] for f in range(8)], nbuf=1)
    mq2 = MMQ(st, sc_banks)
    mq2.free_tok = [exp_done[idx - 1]] * 4 if idx >= 1 else [None] * 4
    adone = attn_done[-1]
    stg_free = {}
    gg = 0
    for f in range(8):
        j = f % 2
        xrt = p.emit("sync", lambda e, f=f, j=j: e.dma_start(
            out=sxr[j][:, :], in_=apx[f * 128:(f + 1) * 128, :]),
            wait=[stg_free.get(gg - 5)], dma=True, dsem=f"sx{j}")
        last = None
        for t4 in range(4):
            pairs = [(wo.buf(f)[:, k * 128:(k + 1) * 128],
                      aT[:, k * S + t4 * 512:k * S + t4 * 512 + 512], None)
                     for k in range(NC)]

            def evs(e, b, par=gg % 2, j=j, t4=t4):
                return e.tensor_add(out=stgo[par][:, :], in0=b[:, :],
                                    in1=sxr[j][:, t4 * 512:(t4 + 1) * 512])
            tok = mq2.group(pairs, [("vector", evs,
                                     [xrt, stg_free.get(gg - 2), adone])],
                            first_wait=[wo.ready(f), adone])
            dt_ = p.emit("sync", lambda e, f=f, t4=t4, par=gg % 2: e.dma_start(
                out=apxo[f * 128:(f + 1) * 128, t4 * 512:(t4 + 1) * 512],
                in_=stgo[par][:, :]), wait=[tok], dma=True, dsem=f"so{gg % 2}")
            stg_free[gg] = dt_
            gg += 1
            last = tok
        wo.release(f, last)
    if DEBUG_IN:
        wda = [attn_done[-1], exp_done[idx - 1]]
        dma_img_out(st, st.outp("dbg_tab", (128, 1024), F32), stab["r1_cs_q"],
                    128, 1024, wait=wda)
        dma_img_out(st, st.outp("dbg_sqsb", (128, 64), F32), sqsb,
                    128, 64, wait=wda)
        apdq = st.outp("dbg_sqs", (1, 256), F32)
        p.emit("sync", lambda e: e.dma_start(out=apdq[:, :], in_=sqs[0:1, :]),
               wait=wda, dma=True)
        dma_img_out(st, st.outp("dbg_tabs", (128, 1024), F32), stab["r1_sn_q"],
                    128, 1024, wait=wda)
        dma_img_out(st, st.outp("dbg_qT", (D, S), BF16), qT, D, S, wait=wda)
        dma_img_out(st, st.outp("dbg_kT", (D, S), BF16), kT, D, S, wait=wda)
        dma_img_out(st, st.outp("dbg_vtok", (S, D), BF16), vtok, S, D, wait=wda)
        dma_img_out(st, st.outp("dbg_aT", (D, S), BF16), aT, D, S, wait=wda)
    st.drain()
    return st.done()


DEBUG_FF = False


@stage("inffn")
def build_inner_ffn():
    st = Stage("inffn")
    p = st.p
    FP = 2816
    NF = 22
    apx = st.inp("x_in", (D, S))
    apw1 = st.inp("w1", (NF, 128, 2048))
    apw2 = st.inp("w2", (8, 128, NF * 128))
    apgb = st.inp("gb2", (2, D))
    apgpp = st.inp("gpp", (128, 8))
    apb1a = st.inp("b1a", (128, NF))
    apb1g = st.inp("b1g", (128, NF))
    apb2 = st.inp("b2pp", (128, 8))
    apoc = st.inp("ones_col", (128, 1))
    apor = st.inp("ones_row", (1, 128))
    apxo = st.outp("x_out", (D, S))

    xn_bf = st.sb("xn_bf", [128, NC * S], BF16)
    h_img = st.sb("h_img", [128, NF * 1024], BF16)
    sxc = [st.sb(f"sxc{j}", [128, S], F32) for j in range(2)]
    sfix = st.sb("sfix", [128, 4 * 512], F32)
    sR4 = st.sb("sR4", [128, S], F32)
    ssq = [st.sb(f"ssqs{j}", [128, 512], F32) for j in range(2)]
    sG = [st.sb(f"sG{j}", [128, 512], F32) for j in range(2)]
    stx = [st.sb(f"stx{j}", [128, 512], F32) for j in range(2)]
    stgo = [st.sb(f"stgo{j}", [128, 512], F32) for j in range(2)]
    smu = st.sb("smu", [1, S], F32)
    sscr = st.sb("sscr", [1, S], F32)
    srow2 = st.sb("srow2", [2, S], F32)
    sgb2 = st.sb("sgb2", [2, D], F32)
    sgpp = st.sb("sgpp", [128, 8], F32)
    sb1a = st.sb("sb1a", [128, NF], F32)
    sb1g = st.sb("sb1g", [128, NF], F32)
    sb2 = st.sb("sb2", [128, 8], F32)
    soc = st.sb("soc", [128, 1], F32)
    sor = st.sb("sor", [1, 128], F32)
    psA = st.ps("psA", [128, 2048])
    psB = st.ps("psB", [128, 2048])
    sA = lambda j: psA[:, j * 512:(j + 1) * 512]
    sB = lambda j: psB[:, j * 512:(j + 1) * 512]

    small = [(apgb, sgb2), (apgpp, sgpp), (apb1a, sb1a), (apb1g, sb1g),
             (apb2, sb2), (apoc, soc), (apor, sor)]
    tsm = []
    for a, s_ in small:
        tsm.append(p.emit("sync", lambda e, a=a, s_=s_: e.dma_start(
            out=s_[:, :], in_=a[:, :]), dma=True, dsem=st.kc()))
    ts_all = tsm
    one_tok = p.emit("vector", lambda e: e.memset(srow2[0:2, :], 1.0))

    def xs_slice(c, t4, par):
        if c < 4:
            return sxc[par][:, t4 * 512:(t4 + 1) * 512]
        if t4 == 0:
            return sfix[:, (c - 4) * 512:(c - 4 + 1) * 512]
        return sxc[par][:, t4 * 512 - 1:(t4 + 1) * 512 - 1]

    fix_tok = {}
    sq_done = {}
    stat_mm_done = {}
    xload = {}
    mmx_tok = mmq_tok = None
    for c in range(NC):
        par = c % 2
        xt = p.emit("sync", lambda e, c=c, par=par: e.dma_start(
            out=sxc[par][:, :], in_=apx[c * 128:(c + 1) * 128, :]),
            wait=[sq_done.get(c - 2), stat_mm_done.get(c - 2)], dma=True,
            dsem=f"sx{par}")
        xload[c] = xt
        if c >= 4:
            cpt = p.emit("vector", lambda e, c=c, par=par: e.tensor_copy(
                out=sfix[:, (c - 4) * 512 + 1:(c - 3) * 512],
                in_=sxc[par][:, 0:511]), wait=[xt])
            fix_tok[c] = p.emit("vector", lambda e, c=c: e.memset(
                sfix[:, (c - 4) * 512:(c - 4) * 512 + 1], 0.0), wait=[])
        for t4 in range(4):
            sqt = p.emit("scalar", lambda e, c=c, t4=t4, par=par: e.activation(
                out=ssq[par][:, :], in_=xs_slice(c, t4, par), func=AF.Square),
                wait=[xt, fix_tok.get(c), mmq_tok])
            if t4 == 3:
                sq_done[c] = sqt
            mmq_tok = p.emit("tensor", lambda e, c=c, t4=t4, par=par: e.matmul(
                psB[0:1, t4 * 512:(t4 + 1) * 512], soc[:, :], ssq[par][:, :],
                start=(c == 0), stop=(c == NC - 1)), wait=[sqt] + ts_all)
            mmx_tok = p.emit("tensor", lambda e, c=c, t4=t4, par=par: e.matmul(
                psA[0:1, t4 * 512:(t4 + 1) * 512], soc[:, :],
                xs_slice(c, t4, par), start=(c == 0), stop=(c == NC - 1)),
                wait=[xt, fix_tok.get(c)])
            stat_mm_done[c] = mmx_tok
    eps_ap, eps_tok = st.cvec(1e-5)
    t_mu = p.emit("scalar", lambda e: e.mul(out=smu[0:1, :], in_=psA[0:1, :],
                                            mul=1.0 / D), wait=[mmx_tok])
    t_mu2 = p.emit("scalar", lambda e: e.activation(out=sscr[0:1, :],
                                                    in_=smu[0:1, :],
                                                    func=AF.Square), wait=[])
    t_var = p.emit("vector", lambda e: e.scalar_tensor_tensor(
        out=sscr[0:1, :], in0=psB[0:1, :], scalar=1.0 / D, in1=sscr[0:1, :],
        op0=ALU.mult, op1=ALU.subtract), wait=[t_mu2, mmq_tok])
    t_std = p.emit("scalar", lambda e, b=eps_ap: e.activation(
        out=sscr[0:1, :], in_=sscr[0:1, :], func=AF.Sqrt, bias=b[0:1, :]),
        wait=[t_var, eps_tok])
    t_r = p.emit("vector", lambda e: e.reciprocal(out=sscr[0:1, :],
                                                  in_=sscr[0:1, :]), wait=[t_std])
    t_ct = p.emit("vector", lambda e: e.scalar_tensor_tensor(
        out=srow2[0:1, :], in0=smu[0:1, :], scalar=-1.0, in1=sscr[0:1, :],
        op0=ALU.mult, op1=ALU.mult), wait=[t_mu, t_r, one_tok])

    rtoks = []
    for t4 in range(4):
        rm = p.emit("tensor", lambda e, t4=t4: e.matmul(
            sA(t4)[0:128, :], sor[:, :], sscr[0:1, t4 * 512:(t4 + 1) * 512],
            start=True, stop=True), wait=[t_r, t_mu])
        rtoks.append(p.emit("vector", lambda e, t4=t4: e.tensor_copy(
            out=sR4[:, t4 * 512:(t4 + 1) * 512], in_=sA(t4)[0:128, :]),
            wait=[rm]))
    cslot_free = {}
    xn_done = {}
    gidx = 0
    for c in range(NC):
        par = c % 2
        xt = p.emit("sync", lambda e, c=c, par=par: e.dma_start(
            out=sxc[par][:, :], in_=apx[c * 128:(c + 1) * 128, :]),
            wait=[xn_done.get(c - 2), mmx_tok, sq_done.get(6), sq_done.get(7)],
            dma=True, dsem=f"sx{par}")
        last = None
        for t4 in range(4):
            sl = gidx % 2
            cm = p.emit("tensor", lambda e, c=c, t4=t4, sl=sl: e.matmul(
                sB(sl)[0:128, :], sgb2[0:2, c * 128:(c + 1) * 128],
                srow2[0:2, t4 * 512:(t4 + 1) * 512], start=True, stop=True),
                wait=[t_ct, one_tok, cslot_free.get(gidx - 2), t_var])
            tx = p.emit("vector", lambda e, c=c, t4=t4, par=par: e.scalar_tensor_tensor(
                out=stx[par][:, :], in0=xs_slice(c, t4, par),
                scalar=sgpp[:, c:c + 1], in1=sR4[:, t4 * 512:(t4 + 1) * 512],
                op0=ALU.mult, op1=ALU.mult),
                wait=[xt, rtoks[t4], fix_tok.get(c)])
            last = p.emit("vector", lambda e, c=c, t4=t4, par=par, sl=sl:
                          e.tensor_add(out=xn_bf[:, c * S + t4 * 512:
                                                 c * S + t4 * 512 + 512],
                                       in0=stx[par][:, :], in1=sB(sl)[0:128, :]),
                          wait=[cm])
            cslot_free[gidx] = last
            gidx += 1
        xn_done[c] = last

    xn_all = [xn_done[NC - 1], xn_done[NC - 2]]

    w1_units = [apw1[f, :, :] for f in range(NF)]
    w2_units = [apw2[o, :, :] for o in range(8)]
    ws1 = WStream(st, "ws1", 2048, w1_units + w1_units)
    ws2 = WStream(st, "ws2", NF * 128, w2_units + w2_units)
    aslot_free = {}
    bslot_free = {}
    h_done = {}
    gi = 0
    out_done = {}
    w2dma = {}
    for th in range(2):
        for fc in range(NF):
            u = th * NF + fc
            last_h = None
            for t4 in range(2):
                toff = th * 1024 + t4 * 512
                sl = gi % 2
                pa = [(ws1.buf(u)[:, k * 128:(k + 1) * 128],
                       xn_bf[:, k * S + toff:k * S + toff + 512], None)
                      for k in range(NC)]
                mmA = None
                for k in range(NC):
                    mmA = p.emit("tensor", lambda e, pa=pa, k=k, sl=sl:
                                 e.matmul(sA(sl)[0:128, :], pa[k][0], pa[k][1],
                                          start=(k == 0), stop=(k == NC - 1)),
                                 wait=([ws1.ready(u), aslot_free.get(gi - 2)]
                                       + xn_all if k == 0 else []))
                mmG = None
                for k in range(NC):
                    mmG = p.emit("tensor", lambda e, u=u, k=k, sl=sl, toff=toff:
                                 e.matmul(sB(sl)[0:128, :],
                                          ws1.buf(u)[:, 1024 + k * 128:1024 + (k + 1) * 128],
                                          xn_bf[:, k * S + toff:k * S + toff + 512],
                                          start=(k == 0), stop=(k == NC - 1)),
                                 wait=([bslot_free.get(gi - 2)] + xn_all
                                       if k == 0 else []))
                par = gi % 2
                gl = p.emit("scalar", lambda e, fc=fc, par=par, sl=sl: e.activation(
                    out=sG[par][:, :], in_=sB(sl)[0:128, :], func=AF.Gelu,
                    bias=sb1g[:, fc:fc + 1]),
                    wait=[mmG, aslot_free.get(gi - 2)] + ts_all)
                bslot_free[gi] = gl
                ht = p.emit("vector", lambda e, fc=fc, par=par, sl=sl, t4=t4, th=th:
                            e.scalar_tensor_tensor(
                                out=h_img[:, fc * 1024 + t4 * 512:fc * 1024 + t4 * 512 + 512],
                                in0=sA(sl)[0:128, :], scalar=sb1a[:, fc:fc + 1],
                                in1=sG[par][:, :], op0=ALU.add, op1=ALU.mult),
                            wait=[gl, mmA])
                aslot_free[gi] = ht
                gi += 1
                last_h = ht
            h_done[(th, fc)] = last_h
            ws1.release(u, last_h)
        hw = [h_done[(th, fc)] for fc in (NF - 1, NF - 2)]
        for o in range(8):
            u = th * 8 + o
            par = o % 2
            xrt = p.emit("sync", lambda e, o=o, par=par: e.dma_start(
                out=sxc[par][:, :], in_=apx[o * 128:(o + 1) * 128, :]),
                wait=[out_done.get((th, o - 2)), xn_done[NC - 1]], dma=True,
                dsem=f"sx{par}")
            last = None
            for t4 in range(2):
                toff = th * 1024 + t4 * 512
                sl = gi % 2
                mm2 = None
                for f in range(NF):
                    mm2 = p.emit("tensor", lambda e, u=u, f=f, sl=sl, t4=t4:
                                 e.matmul(sA(sl)[0:128, :],
                                          ws2.buf(u)[:, f * 128:(f + 1) * 128],
                                          h_img[:, f * 1024 + t4 * 512:
                                                f * 1024 + t4 * 512 + 512],
                                          start=(f == 0), stop=(f == NF - 1)),
                                 wait=([ws2.ready(u), aslot_free.get(gi - 2)] + hw
                                       if f == 0 else []))
                ot = p.emit("vector", lambda e, o=o, par=par, sl=sl, t4=t4, th=th:
                            e.scalar_tensor_tensor(
                                out=stgo[sl][:, :], in0=sA(sl)[0:128, :],
                                scalar=sb2[:, o:o + 1],
                                in1=sxc[par][:, th * 1024 + t4 * 512:th * 1024 + t4 * 512 + 512],
                                op0=ALU.add, op1=ALU.add),
                            wait=[mm2, xrt, w2dma.get(gi - 2)])
                aslot_free[gi] = ot
                last = p.emit("sync", lambda e, o=o, th=th, t4=t4, sl=sl:
                              e.dma_start(out=apxo[o * 128:(o + 1) * 128,
                                                   th * 1024 + t4 * 512:th * 1024 + t4 * 512 + 512],
                                          in_=stgo[sl][:, :]), wait=[ot],
                                          dma=True, dsem=f"so{sl}")
                w2dma[gi] = last
                gi += 1
            out_done[(th, o)] = last
            ws2.release(u, last)
    if DEBUG_FF:
        wda = [out_done[(1, 7)], xn_done[NC - 1]]
        dma_img_out(st, st.outp("dbg_xn", (D, S), BF16), xn_bf, D, S, wait=wda)
        dma_img_out(st, st.outp("dbg_h", (NF * 128, 1024), BF16), h_img,
                    NF * 128, 1024, wait=wda)
        aps1 = st.outp("dbg_rows", (2, S))
        p.emit("sync", lambda e: e.dma_start(out=aps1[:, :], in_=srow2[0:2, :]),
               wait=wda, dma=True)
        aps2 = st.outp("dbg_r", (128, S))
        p.emit("sync", lambda e: e.dma_start(out=aps2[:, :], in_=sR4[:, :]),
               wait=wda, dma=True)
        aps3 = st.outp("dbg_mu", (1, S))
        p.emit("sync", lambda e: e.dma_start(out=aps3[:, :], in_=smu[0:1, :]),
               wait=wda, dma=True)
        aps4 = st.outp("dbg_rr", (1, S))
        p.emit("sync", lambda e: e.dma_start(out=aps4[:, :], in_=sscr[0:1, :]),
               wait=wda, dma=True)
    st.drain()
    return st.done()


DEBUG_L1 = False


@stage("l1attn")
def build_l1attn():
    st = Stage("l1attn")
    p = st.p
    apxT = st.inp("xnT_bf", (D, S), BF16)
    apw = {w: st.inp("w" + w, (2, 128, 8 * 512)) for w in ("q", "k", "v")}
    apwo = st.inp("wo", (8, 128, 1024))
    apbq = st.inp("bq", (1, D))
    apbk = st.inp("bk", (1, D))
    apbv = st.inp("bv", (1, D))
    apbo = st.inp("bo", (128, 8))
    apid = st.inp("id_bf", (128, 128), BF16)
    apml = st.inp("mask_low", (128, 128))
    apmu = st.inp("mask_upp", (128, 128))
    apor = st.inp("ones_row", (1, 128))
    apor8 = st.inp("ones8_row", (1, 128))
    apsT = st.outp("s1T", (D, S), BF16)

    ximg = st.sb("ximg", [128, NC * S], BF16)
    qT = st.sb("qT", [128, NC * S], BF16)
    kT = st.sb("kT", [128, NC * S], BF16)
    vtok = st.sb("vtok", [128, NT * D], BF16)
    aT = ximg
    sid = st.sb("sid", [128, 128], BF16)
    sml = st.sb("sml", [128, 128], F32)
    smu_ = st.sb("smu_", [128, 128], F32)
    sbr = st.sb("sbr", [1, 3 * D], F32)
    sor = st.sb("sor", [1, 256], F32)
    sbb = {w: st.sb(f"sbb{w}", [128, D], F32) for w in ("q", "k", "v")}
    sbo = st.sb("sbo", [128, 8], F32)
    qrot = [st.sb(f"qrot{j}", [128, 512], BF16) for j in range(2)]
    sPex = st.sb("sPex", [128, 256], BF16)
    sPT = st.sb("sPT", [128, 256], BF16)
    srv = st.sb("srv", [128, 8], F32)
    stgs = [st.sb(f"stgs{j}", [128, 512], BF16) for j in range(2)]
    ps_sc = st.ps("ps_sc", [128, 2048])
    ps_t = [st.ps(f"ps_t{j}", [128, 1024], BF16) for j in range(2)]
    ps_av = [st.ps(f"ps_av{j}", [64, 512]) for j in range(2)]

    xt = dma_img_in(st, ximg, apxT, D, S, dsem="xi")
    tid = p.emit("sync", lambda e: e.dma_start(out=sid[:, :], in_=apid[:, :]),
                 dma=True, dsem=st.kc())
    tml = p.emit("sync", lambda e: e.dma_start(out=sml[:, :], in_=apml[:, :]),
                 dma=True, dsem=st.kc())
    tmu = p.emit("sync", lambda e: e.dma_start(out=smu_[:, :], in_=apmu[:, :]),
                 dma=True, dsem=st.kc())
    tbo = p.emit("sync", lambda e: e.dma_start(out=sbo[:, :], in_=apbo[:, :]),
                 dma=True, dsem=st.kc())
    tb = {}
    for ix, (w, a) in enumerate([("q", apbq), ("k", apbk), ("v", apbv)]):
        tb[w] = p.emit("sync", lambda e, ix=ix, a=a: e.dma_start(
            out=sbr[0:1, ix * D:(ix + 1) * D], in_=a[:, :]), dma=True, dsem=st.kc())
    to1 = p.emit("sync", lambda e: e.dma_start(out=sor[0:1, 0:128], in_=apor[:, :]),
                 dma=True, dsem=st.kc())
    to8 = p.emit("sync", lambda e: e.dma_start(out=sor[0:1, 128:256], in_=apor8[:, :]),
                 dma=True, dsem=st.kc())
    sc_banks = [ps_sc[:, j * 512:(j + 1) * 512] for j in range(4)]
    btok = {}
    prev = None
    for ix, w in enumerate(("q", "k", "v")):
        lh = sor[0:1, 128:256] if w == "q" else sor[0:1, 0:128]
        for t in range(2):
            bm = p.emit("tensor", lambda e, ix=ix, t=t, lh=lh: e.matmul(
                sc_banks[0][0:128, :], lh,
                sbr[0:1, ix * D + t * 512:ix * D + (t + 1) * 512],
                start=True, stop=True),
                wait=[tb[w], to1, to8, prev])
            prev = p.emit("vector", lambda e, w=w, t=t: e.tensor_copy(
                out=sbb[w][:, t * 512:(t + 1) * 512], in_=sc_banks[0][0:128, :]),
                wait=[bm])
            btok[w] = prev

    units = [apw[w][n, :, :] for w in ("q", "k", "v") for n in range(2)]
    ws = WStream(st, "wqkv", 8 * 512, units)
    mq = MMQ(st, sc_banks)
    mq.free_tok[0] = prev
    tp_par = 0
    tp_free = {}
    ev_free = {}
    qkv_end = []
    for wi, w in enumerate(("q", "k", "v")):
        for n in range(2):
            u = wi * 2 + n
            for i in range(NT):
                pairs = [(ximg[:, k * S + i * 128:k * S + i * 128 + 128],
                          ws.buf(u)[:, k * 512:(k + 1) * 512], None)
                         for k in range(NC)]
                if w == "v":
                    def ev(e, b, i=i, n=n):
                        return e.tensor_tensor(
                            out=vtok[:, i * D + n * 512:i * D + n * 512 + 512],
                            in0=b[:, :], in1=sbb["v"][:, n * 512:(n + 1) * 512],
                            op=ALU.add)
                    tok = mq.group(pairs, [("vector", ev, [btok["v"]])],
                                   first_wait=[ws.ready(u), xt])
                    qkv_end.append(tok)
                else:
                    par = i % 2

                    def evq(e, b, w=w, n=n, par=par):
                        if w == "q":
                            return e.scalar_tensor_tensor(
                                out=qrot[par][:, :], in0=b[:, :], scalar=0.125,
                                in1=sbb["q"][:, n * 512:(n + 1) * 512],
                                op0=ALU.mult, op1=ALU.add)
                        return e.tensor_tensor(
                            out=qrot[par][:, :], in0=b[:, :],
                            in1=sbb["k"][:, n * 512:(n + 1) * 512], op=ALU.add)
                    tok = mq.group(pairs, [("vector", evq,
                                            [btok[w], ev_free.get((w, i - 2))])],
                                   first_wait=[ws.ready(u), xt])
                    ev_free[(w, i)] = tok
                    dst = qT if w == "q" else kT
                    tp = None
                    for k in range(4):
                        tp = p.emit("tensor", lambda e, par=par, k=k, tpp=tp_par % 2:
                                    e.transpose(ps_t[tpp][:, k * 128:(k + 1) * 128],
                                                qrot[par][:, k * 128:(k + 1) * 128],
                                                sid[:, :]),
                                    wait=([tok, tid, tp_free.get(tp_par - 2)]
                                          if k == 0 else []))
                    dap = dst[:, :].rearrange("p (c t) -> p c t", t=S)[
                        :, n * 4:n * 4 + 4, i * 128:(i + 1) * 128]
                    tev = p.emit("vector", lambda e, tpp=tp_par % 2, dap=dap:
                                 e.tensor_copy(out=dap, in_=ps_t[tpp][:, 0:512]
                                               .rearrange("p (c t) -> p c t", t=128)),
                                 wait=[tp])
                    tp_free[tp_par] = tev
                    tp_par += 1
                    qkv_end.append(tev)
            ws.release(u, qkv_end[-1])

    qkv_wait = {}
    for t in qkv_end:
        qkv_wait[t[0]] = max(qkv_wait.get(t[0], 0), t[1])
    qkv_wait = list(qkv_wait.items())
    attn_done = []
    exp_done = {}
    av_free = {}
    idx = 0
    for h in range(16):
        base = (h // 2) * S
        row = (h % 2) * 64
        for qt in range(NT):
            kt0 = max(0, qt - 1)
            span = qt + 1 - kt0
            L = span * 128
            par4 = idx % 4
            scs = ps_sc[:, par4 * 512:par4 * 512 + 512]
            wf = [exp_done.get(idx - 4)] + (qkv_wait if idx == 0 else [])
            mt = p.emit("tensor", lambda e, row=row, base=base, qt=qt, kt0=kt0,
                        L=L, scs=scs:
                        e.matmul(scs[:, 0:L],
                                 qT[row:row + 64, base + qt * 128:base + qt * 128 + 128],
                                 kT[row:row + 64, base + kt0 * 128:base + kt0 * 128 + L],
                                 start=True, stop=True), wait=wf)
            mk1 = p.emit("vector", lambda e, scs=scs, L=L: e.tensor_tensor(
                out=scs[:, L - 128:L], in0=scs[:, L - 128:L], in1=sml[:, :],
                op=ALU.add), wait=[mt, tml])
            if span == 2:
                mk1 = p.emit("vector", lambda e, scs=scs: e.tensor_tensor(
                    out=scs[:, 0:128], in0=scs[:, 0:128], in1=smu_[:, :],
                    op=ALU.add), wait=[tmu])
            par = idx % 2
            nmx = p.emit("vector", lambda e, scs=scs, L=L, par=par:
                         e.reduce_max(out=srv[:, 4 * par:4 * par + 1],
                                      in_=scs[:, 0:L], axis=AX.X, negate=True),
                         wait=[mk1])
            ext = p.emit("scalar", lambda e, scs=scs, L=L, par=par:
                         e.activation(out=sPex[:, 0:L], in_=scs[:, 0:L],
                                      func=AF.Exp, bias=srv[:, 4 * par:4 * par + 1],
                                      accum_out=srv[:, 4 * par + 1:4 * par + 2]),
                         wait=[nmx, exp_done.get(idx - 1)])
            exp_done[idx] = ext
            rcp = p.emit("vector", lambda e, par=par: e.reciprocal(
                out=srv[:, 4 * par + 2:4 * par + 3],
                in_=srv[:, 4 * par + 1:4 * par + 2]), wait=[ext])
            scl = p.emit("vector", lambda e, L=L, par=par: e.tensor_scalar(
                out=sPex[:, 0:L], in0=sPex[:, 0:L],
                scalar1=srv[:, 4 * par + 2:4 * par + 3], scalar2=None,
                op0=ALU.mult), wait=[rcp])
            tp = None
            for kt in range(span):
                tp = p.emit("tensor", lambda e, kt=kt, tpp=tp_par % 2:
                            e.transpose(ps_t[tpp][:, kt * 128:(kt + 1) * 128],
                                        sPex[:, kt * 128:(kt + 1) * 128], sid[:, :]),
                            wait=([scl, tp_free.get(tp_par - 2)] if kt == 0 else []))
            ev2 = p.emit("vector", lambda e, span=span, tpp=tp_par % 2:
                         e.tensor_copy(out=sPT[:, 0:span * 128],
                                       in_=ps_t[tpp][:, 0:span * 128]), wait=[tp])
            tp_free[tp_par] = ev2
            tp_par += 1
            avt = None
            for kt in range(span):
                ka = kt0 + kt
                wf2 = [ev2, av_free.get(idx - 2)] if kt == 0 else []
                avt = p.emit("tensor", lambda e, ka=ka, kt=kt, h=h, par=par, span=span:
                             e.matmul(ps_av[par][0:64, 0:128],
                                      vtok[:, ka * D + h * 64:ka * D + h * 64 + 64],
                                      sPT[:, kt * 128:(kt + 1) * 128],
                                      start=(kt == 0), stop=(kt == span - 1)),
                             wait=wf2)
            at = p.emit("scalar", lambda e, par=par, row=row, base=base, qt=qt:
                        e.copy(out=aT[row:row + 64, base + qt * 128:base + qt * 128 + 128],
                               in_=ps_av[par][0:64, 0:128]), wait=[avt])
            av_free[idx] = at
            attn_done.append(at)
            idx += 1

    wo = WStream(st, "wo", 1024, [apwo[f, :, :] for f in range(8)])
    mq2 = MMQ(st, sc_banks)
    mq2.free_tok = [exp_done[idx - 1]] * 4
    adone = attn_done[-1]
    stg_free = {}
    gg = 0
    for f in range(8):
        last = None
        for t4 in range(4):
            pairs = [(wo.buf(f)[:, k * 128:(k + 1) * 128],
                      aT[:, k * S + t4 * 512:k * S + t4 * 512 + 512], None)
                     for k in range(NC)]

            def evs(e, b, par=gg % 2, f=f):
                return e.activation(out=stgs[par][:, :], in_=b[:, :],
                                    func=AF.Identity, bias=sbo[:, f:f + 1],
                                    scale=1.0)
            tok = mq2.group(pairs, [("scalar", evs,
                                     [stg_free.get(gg - 2), adone, tbo])],
                            first_wait=[wo.ready(f), adone])
            dt_ = p.emit("sync", lambda e, f=f, t4=t4, par=gg % 2: e.dma_start(
                out=apsT[f * 128:(f + 1) * 128, t4 * 512:(t4 + 1) * 512],
                in_=stgs[par][:, :]), wait=[tok], dma=True, dsem=f"so{gg % 2}")
            stg_free[gg] = dt_
            gg += 1
            last = tok
        wo.release(f, last)
    if DEBUG_L1:
        wda = [attn_done[-1], exp_done[idx - 1]]
        dma_img_out(st, st.outp("dbg_qT", (D, S), BF16), qT, D, S, wait=wda)
        dma_img_out(st, st.outp("dbg_kT", (D, S), BF16), kT, D, S, wait=wda)
        dma_img_out(st, st.outp("dbg_vtok", (S, D), BF16), vtok, S, D, wait=wda)
        dma_img_out(st, st.outp("dbg_aT", (D, S), BF16), aT, D, S, wait=wda)
        dma_img_out(st, st.outp("dbg_bq", (128, D), F32), sbb["q"], 128, D, wait=wda)
    st.drain()
    return st.done()

_CONSTS = None


def kernel(**inputs):
    global _CONSTS
    if _CONSTS is None:
        _CONSTS = make_consts()
    C = _CONSTS
    W = pack_weights(inputs)
    x = np.asarray(inputs["L0_attn_x"], np.float32)
    B = x.shape[0]
    NCORE = 8
    spc = B // NCORE
    out = np.zeros_like(x)

    def launch(stg, im_fn):
        res = run_bass_kernel_spmd(stg.nc, [im_fn(c) for c in range(NCORE)],
                                   core_ids=list(range(NCORE)))
        return res.results

    for s in range(spc):
        xs = [np.ascontiguousarray(x[c * spc + s]) for c in range(NCORE)]
        r = launch(build_ln(), lambda c: {"x": xs[c], "id_bf": C["id_bf"]})
        xnT = [np.asarray(r[c]["xnT_bf"]) for c in range(NCORE)]
        r = launch(build_router(), lambda c: {
            "xnT_bf": xnT[c], "rg": W["l0_rg"], "id_f32": C["id_f32"]})
        cwT = [np.asarray(r[c]["cwT"]) for c in range(NCORE)]
        acc = [[np.zeros((D, S // 2), np.float32) for _ in range(2)]
               for c in range(NCORE)]
        for hh in range(2):
            xh = [np.ascontiguousarray(xnT[c][:, hh * (S // 2):(hh + 1) * (S // 2)])
                  for c in range(NCORE)]
            for e in range(8):
                r = launch(build_moe(), lambda c, e=e, hh=hh: {
                    "xnT_bf": xh[c],
                    "cw_row": np.ascontiguousarray(
                        cwT[c][e:e + 1, hh * (S // 2):(hh + 1) * (S // 2)]),
                    "acc_in": acc[c][hh], "wgu": W[f"moe_gu_{e}"],
                    "wd": W[f"moe_d_{e}"], "ones_row": C["ones_row"]})
                for c in range(NCORE):
                    acc[c][hh] = np.asarray(r[c]["acc_out"])
        r = launch(build_l0attn(), lambda c: {
            "xnT_bf": xnT[c], "wq": W["l0_W_q"], "wk": W["l0_W_k"],
            "wv": W["l0_W_v"], "wo": W["l0_W_o"], "id_bf": C["id_bf"],
            "mask_low": C["mask_low"], "r0_cos_q": C["r0_cos_q"],
            "r0_sin_q": C["r0_sin_q"], "r0_cos_k": C["r0_cos_k"],
            "r0_sin_k": C["r0_sin_k"]})
        sT = [np.asarray(r[c]["sT"]) for c in range(NCORE)]
        r = launch(build_combine(), lambda c: {
            "x": xs[c], "sT": sT[c],
            "acc": np.ascontiguousarray(np.concatenate(acc[c], axis=1)),
            "id_bf": C["id_bf"]})
        x1 = [np.asarray(r[c]["xo"]) for c in range(NCORE)]
        r = launch(build_ln(), lambda c: {"x": x1[c], "id_bf": C["id_bf"]})
        xn1T = [np.asarray(r[c]["xnT_bf"]) for c in range(NCORE)]
        xin = [np.asarray(r[c]["xnT_f32"]) for c in range(NCORE)]
        r = launch(build_l1attn(), lambda c: {
            "xnT_bf": xn1T[c], "wq": W["l1_Wq"], "wk": W["l1_Wk"],
            "wv": W["l1_Wv"], "wo": W["l1_Wo"], "bq": W["l1_bq"],
            "bk": W["l1_bk"], "bv": W["l1_bv"], "bo": W["l1_bo"],
            "id_bf": C["id_bf"], "mask_low": C["mask_low"],
            "mask_upp": C["mask_upp"], "ones_row": C["ones_row"],
            "ones8_row": C["ones8_row"]})
        s1T = [np.asarray(r[c]["s1T"]) for c in range(NCORE)]
        for li in range(6):
            r = launch(build_inner_attn(), lambda c, li=li: {
                "x_in": xin[c], "wqkv": W[f"in_qkv_{li}"], "wo": W[f"in_wo_{li}"],
                "qs64": W[f"in_qs_{li}"], "qs64r": W[f"in_qsr_{li}"],
                "ks64": W[f"in_ks_{li}"], "ks64r": W[f"in_ksr_{li}"],
                "r1_cs_q": C["r1_cs_q"], "r1_sn_q": C["r1_sn_q"],
                "r1_cs_k": C["r1_cs_k"], "r1_sn_k": C["r1_sn_k"],
                "id_bf": C["id_bf"], "mask_low": C["mask_low"],
                "mask_upp": C["mask_upp"], "ones_row": C["ones_row"]})
            xin = [np.asarray(r[c]["x_out"]) for c in range(NCORE)]
            r = launch(build_inner_ffn(), lambda c, li=li: {
                "x_in": xin[c], "w1": W[f"in_w1_{li}"], "w2": W[f"in_w2_{li}"],
                "gb2": W[f"in_gb_{li}"], "gpp": W[f"in_gpp_{li}"],
                "b1a": W[f"in_b1a_{li}"], "b1g": W[f"in_b1g_{li}"],
                "b2pp": W[f"in_b2_{li}"], "ones_col": C["ones_col"],
                "ones_row": C["ones_row"]})
            xin = [np.asarray(r[c]["x_out"]) for c in range(NCORE)]
        r = launch(build_combine(), lambda c: {
            "x": x1[c], "sT": s1T[c], "acc": xin[c], "id_bf": C["id_bf"]})
        for c in range(NCORE):
            out[c * spc + s] = np.asarray(r[c]["xo"])
    return out
